# version 10
# baseline (speedup 1.0000x reference)
import numpy as np
import concourse.bass as bass
import concourse.mybir as mybir
from concourse.bass_utils import run_bass_kernel_spmd
from contextlib import ExitStack

F32 = mybir.dt.float32
BF16 = mybir.dt.bfloat16
I32 = mybir.dt.int32
AF = mybir.ActivationFunctionType
ALU = mybir.AluOpType
AX = mybir.AxisListType
IOA = bass.IndirectOffsetOnAxis

D = 1024
NF = 8
TC = 4096
NT = 32
CH = 512
NCH = 8
E = 8
DI = 2048
NI = 16
CAP = 4096
PAD = 640
NST = PAD // 128
TOK = 32768
HEADS = 16
HD = 64
EPS = 1e-5
NEG = -1.0e30
OOB = 1.0e9


class Sem:
    def __init__(self, stack, nc, name):
        self.h = stack.enter_context(nc.semaphore(name))
        self.n = 0

    def inc(self, instr, k=1):
        instr.then_inc(self.h, k)
        self.n += k
        return self.n

    def dma(self, instr):
        return self.inc(instr, 16)


class Ctx:
    pass


_CHAIN_OPS = {
    "memset", "memzero", "tensor_copy", "tensor_tensor", "tensor_add",
    "tensor_sub", "tensor_mul", "tensor_max", "tensor_scalar",
    "tensor_single_scalar", "tensor_scalar_add", "tensor_scalar_sub",
    "tensor_scalar_mul", "tensor_scalar_max", "tensor_scalar_min",
    "tensor_relu", "tensor_reduce", "reduce_max", "reduce_sum",
    "scalar_tensor_tensor", "tensor_tensor_scan", "reciprocal", "select",
    "engine_nop", "activation", "copy", "mul", "add", "square", "sqrt",
    "sign", "iota",
}


class EP:

    def __init__(self, eng, sem):
        self._eng = eng
        self._sem = sem

    def __getattr__(self, name):
        fn = getattr(self._eng, name)
        if name not in _CHAIN_OPS:
            return fn
        sem = self._sem
        eng = self._eng

        def wrapped(*a, **k):
            if sem.n > 0:
                eng.wait_ge(sem.h, sem.n)
            ins = fn(*a, **k)
            sem.inc(ins, 1)
            return ins
        return wrapped


def mk_ctx(nc, st, block):
    c = Ctx()
    c.nc = nc
    c.T = block.tensor
    c.G = block.gpsimd
    c.Y = block.sync
    c.sY = Sem(st, nc, "sY")
    c.sA = Sem(st, nc, "sA")
    c.sG = Sem(st, nc, "sG")
    c.cP = Sem(st, nc, "cP")
    c.cV = Sem(st, nc, "cV")
    c.cA = Sem(st, nc, "cA")

    def V(f):
        block.vector(lambda eng: f(EP(eng, c.cV)))

    def A(f):
        block.scalar(lambda eng: f(EP(eng, c.cA)))
    c.V = V
    c.A = A
    return c


def barrier(c):
    snap = [(s.h, s.n) for s in (c.sY, c.sA, c.sG, c.cP, c.cV, c.cA)]

    def w(eng):
        for h, n in snap:
            if n > 0:
                eng.wait_ge(h, n)
    c.T(w)
    c.V(w)
    c.A(w)
    c.G(w)
    c.Y(w)


def emit_stats(c, xap, rows, scr, pG, pH, pre_waits=()):
    pe0 = c.cP.n
    aq0 = c.cA.n

    def act_sq(scalar):
        for sem, val in pre_waits:
            scalar.wait_ge(sem.h, val)
        for s in range(NF):
            if s >= 2:
                scalar.wait_ge(c.cP.h, pe0 + 1 + s - 1)
            scalar.activation(scr[s % 2][:], xap(s), AF.Square)
    c.A(act_sq)
    sq_ca = c.cA.n

    def pe_stats(tensor):
        for sem, val in pre_waits:
            tensor.wait_ge(sem.h, val)
        for s in range(NF):
            ins = tensor.matmul(pG[0:1, 0:CH], c.ones_col[:, 0:1], xap(s),
                                start=(s == 0), stop=(s == NF - 1))
        c.cP.inc(ins)
        for s in range(NF):
            tensor.wait_ge(c.cA.h, aq0 + s + 1)
            c.cP.inc(tensor.matmul(pH[0:1, 0:CH], c.ones_col[:, 0:1],
                                   scr[s % 2][:],
                                   start=(s == 0), stop=(s == NF - 1)))
    c.T(pe_stats)
    pe_done = c.cP.n

    def act_evac(scalar):
        scalar.wait_ge(c.cP.h, pe_done)
        scalar.copy(rows["sum"], pG[0:1, 0:CH])
        scalar.activation(rows["mean"], pG[0:1, 0:CH], AF.Copy, scale=1.0 / D)
        scalar.copy(rows["sumsq"], pH[0:1, 0:CH])
    c.A(act_evac)
    a1 = c.cA.n

    def v_var(vector):
        vector.wait_ge(c.cA.h, a1)
        vector.tensor_scalar_mul(rows["tmp"], rows["sumsq"], 1.0 / D)
        vector.tensor_tensor(rows["tmp2"], rows["mean"], rows["mean"], ALU.mult)
        vector.tensor_sub(rows["tmp"], rows["tmp"], rows["tmp2"])
    c.V(v_var)
    v1 = c.cV.n

    def act_rstd(scalar):
        scalar.wait_ge(c.cV.h, v1)
        scalar.activation(rows["tmp2"], rows["tmp"], AF.Ln, bias=c.epsb)
        scalar.activation(rows["rstd"], rows["tmp2"], AF.Exp, scale=-0.5)
    c.A(act_rstd)
    a2 = c.cA.n

    def v_mr(vector):
        vector.wait_ge(c.cA.h, a2)
        vector.scalar_tensor_tensor(rows["negmrstd"], rows["mean"], -1.0,
                                             rows["rstd"], ALU.mult, ALU.mult)
    c.V(v_mr)
    return dict(cv=c.cV.n, sq=sq_ca, evac=a1, stats_pe=pe_done)


def emit_ln_apply(c, xap, rows, xn, stats_cv, pE, pF, pe_pre=(), v_pre=()):

    def pe_b(tensor):
        tensor.wait_ge(c.cV.h, stats_cv)
        for sem, val in pe_pre:
            tensor.wait_ge(sem.h, val)
        c.cP.inc(tensor.matmul(pE[:, 0:CH], c.ones_row[0:1, :], rows["rstd"],
                               start=True, stop=True))
        c.cP.inc(tensor.matmul(pF[:, 0:CH], c.ones_row[0:1, :],
                               rows["negmrstd"], start=True, stop=True))
    c.T(pe_b)
    p1 = c.cP.n

    def v_ap(vector):
        vector.wait_ge(c.cP.h, p1)
        for sem, val in v_pre:
            vector.wait_ge(sem.h, val)
        for s in range(NF):
            vector.tensor_tensor(c.t1[:], xap(s), pE[:, 0:CH], ALU.mult)
            vector.tensor_tensor(xn[:, s * CH:(s + 1) * CH], c.t1[:],
                                 pF[:, 0:CH], ALU.add)
        vector.engine_nop()
    c.V(v_ap)
    return c.cV.n


def emit_logits(c, xap, ch, rows, stats_cv, lgt_local, pH, pI, pJ):
    for jt in range(4):
        tile = ch * 4 + jt
        tt = jt * 128

        def pe_t(tensor, tile=tile, tt=tt):
            tensor.wait_ge(c.cV.h, stats_cv)
            tensor.wait_ge(c.cV.h, c.cV.n)
            for s in range(NF):
                tensor.matmul(pH[:, 0:8], xap(s)[:, tt:tt + 128],
                              c.eeT[:, s * 8:(s + 1) * 8],
                              start=(s == 0), stop=False)
            c.cP.inc(tensor.matmul(pH[:, 0:8], rows["mean"][0:1, tt:tt + 128],
                                   c.negs8[0:1, 0:8], start=False, stop=True))
            c.cP.inc(tensor.transpose(pI[:, 0:1], rows["mean"][0:1, tt:tt + 128],
                                      c.identf[0:1, 0:1]))
            c.cP.inc(tensor.transpose(pJ[:, 0:1], rows["rstd"][0:1, tt:tt + 128],
                                      c.identf[0:1, 0:1]))
        c.T(pe_t)
        p1 = c.cP.n

        def v_t(vector, tile=tile, p1=p1):
            vector.wait_ge(c.cP.h, p1)
            vector.tensor_scalar_mul(
                lgt_local[:, tile * 8:(tile + 1) * 8], pH[:, 0:8], pJ[:, 0:1])
        c.V(v_t)
    return dict(cv=c.cV.n, pe=c.cP.n)


def emit_routing(c, rt, pG, t16_d):
    lgt_v = rt["lgt"][:].rearrange("p (j e) -> p j e", e=8)

    def v1(vector):
        vector.memset(rt["ones256"][:], 1.0)
        vector.reduce_max(rt["m1"][:], lgt_v, axis=AX.X)
        m1b = rt["m1"][:].unsqueeze(2).broadcast_to([128, 256, 8])
        l2v = rt["l2"][:].rearrange("p (j e) -> p j e", e=8)
        vector.tensor_tensor(l2v, lgt_v, m1b, ALU.is_equal)
        vector.tensor_scalar_mul(rt["w"][:], rt["l2"][:], NEG)
        vector.tensor_tensor(rt["msk2"][:], rt["lgt"][:], rt["w"][:], ALU.add)
        m2v = rt["msk2"][:].rearrange("p (j e) -> p j e", e=8)
        vector.reduce_max(rt["m2"][:], m2v, axis=AX.X)
        vector.tensor_sub(rt["d21"][:], rt["m2"][:], rt["m1"][:])
    c.V(v1)
    m1 = c.cV.n

    def a1(scalar):
        scalar.wait_ge(c.cV.h, m1)
        scalar.activation(rt["e21"][:], rt["d21"][:], AF.Exp)
    c.A(a1)
    a1n = c.cA.n

    def v2(vector):
        vector.wait_ge(c.cA.h, a1n)
        vector.tensor_scalar_add(rt["tmp"][:], rt["e21"][:], 1.0)
        vector.reciprocal(rt["w1"][:], rt["tmp"][:])
        vector.tensor_tensor(rt["w2"][:], rt["e21"][:], rt["w1"][:], ALU.mult)
        l2v = rt["l2"][:].rearrange("p (j e) -> p j e", e=8)
        msk2v = rt["msk2"][:].rearrange("p (j e) -> p j e", e=8)
        m2b = rt["m2"][:].unsqueeze(2).broadcast_to([128, 256, 8])
        w1b = rt["w1"][:].unsqueeze(2).broadcast_to([128, 256, 8])
        w2b = rt["w2"][:].unsqueeze(2).broadcast_to([128, 256, 8])
        wv = rt["w"][:].rearrange("p (j e) -> p j e", e=8)
        vector.tensor_tensor(msk2v, msk2v, m2b, ALU.is_equal)
        vector.tensor_tensor(wv, l2v, w1b, ALU.mult)
        vector.tensor_tensor(l2v, msk2v, w2b, ALU.mult)
        vector.tensor_add(rt["w"][:], rt["w"][:], rt["l2"][:])
        vector.tensor_single_scalar(rt["l2"][:], rt["w"][:], 0.0, ALU.is_gt)
        vector.tensor_copy(rt["msk2"][:].rearrange("p (e j) -> p j e", e=8),
                           rt["l2"][:].rearrange("p (j e) -> p j e", e=8))
        for e in range(E):
            vector.tensor_tensor_scan(rt["cs2"][:, e * 256:(e + 1) * 256],
                                      rt["ones256"][:],
                                      rt["msk2"][:, e * 256:(e + 1) * 256],
                                      0.0, ALU.mult, ALU.add)
        vector.tensor_copy(
            rt["totb"][:],
            rt["cs2"][:].rearrange("p (e j) -> p e j", j=256)[:, :, 255])
    c.V(v2)
    v2n = c.cV.n

    def pe1(tensor):
        tensor.wait_ge(c.cV.h, v2n)
        c.cP.inc(tensor.matmul(pG[:, 0:8], c.strictU[:], rt["totb"][:],
                               start=True, stop=True))
    c.T(pe1)
    p1 = c.cP.n

    def a2(scalar):
        scalar.wait_ge(c.cP.h, p1)
        scalar.copy(rt["pfx"][:], pG[:, 0:8])
    c.A(a2)
    a2n = c.cA.n

    def v3(vector):
        vector.wait_ge(c.cA.h, a2n)
        for e in range(E):
            vector.tensor_scalar_add(rt["cs2"][:, e * 256:(e + 1) * 256],
                                     rt["cs2"][:, e * 256:(e + 1) * 256],
                                     rt["pfx"][:, e:e + 1])
        vector.tensor_single_scalar(rt["t2048"][:], rt["cs2"][:], float(CAP),
                                    ALU.is_le)
        vector.tensor_tensor(rt["t2048"][:], rt["t2048"][:], rt["msk2"][:],
                             ALU.mult)
        vector.tensor_copy(rt["msk2"][:].rearrange("p (j e) -> p e j", j=256),
                           rt["t2048"][:].rearrange("p (e j) -> p e j", j=256))
        vector.tensor_tensor(rt["w"][:], rt["w"][:], rt["msk2"][:], ALU.mult)
        vector.tensor_copy(rt["t2048"][:].rearrange("p (j e) -> p e j", j=256),
                           rt["cs2"][:].rearrange("p (e j) -> p e j", j=256))
        vector.engine_nop()
    c.V(v3)
    v3n = c.cV.n

    def yw(sync):
        sync.wait_ge(c.cV.h, v3n)
        c.sY.dma(sync.dma_start(
            out=t16_d.rearrange("(p j) f -> p j f", p=128)[:, :, 0:8],
            in_=rt["w"][:].rearrange("p (j e) -> p j e", e=8)))
        c.sY.dma(sync.dma_start(
            out=t16_d.rearrange("(p j) f -> p j f", p=128)[:, :, 8:16],
            in_=rt["t2048"][:].rearrange("p (j e) -> p j e", e=8)))
    c.Y(yw)
    return c.sY.n


def emit_mine(c, mn, pG, t16_d, t16_done, sg_loads):
    def g1(gpsimd):
        gpsimd.wait_ge(c.sY.h, t16_done)
        gpsimd.wait_ge(c.sG.h, sg_loads)
        for j in range(NT):
            c.sG.dma(gpsimd.indirect_dma_start(
                mn["mine"][:, j * 16:(j + 1) * 16], None,
                t16_d[:, :], IOA(c.gidx[:, j:j + 1], 0)))
    c.G(g1)
    g1n = c.sG.n

    def v1(vector):
        vector.wait_ge(c.sG.h, g1n)
        vector.memset(mn["ones32"][:], 1.0)
        mv = mn["mine"][:].rearrange("p (j f) -> p j f", f=16)
        kmv = mn["km"][:].rearrange("p (j e) -> p j e", e=8)
        vector.tensor_single_scalar(kmv, mv[:, :, 0:8], 0.0, ALU.is_gt)
        kcv = mn["kc"][:].rearrange("p (j e) -> p j e", e=8)
        vector.tensor_copy(kcv[:, :, 0:1], kmv[:, :, 0:1])
        for e in range(1, E):
            vector.tensor_tensor(kcv[:, :, e:e + 1], kcv[:, :, e - 1:e],
                                 kmv[:, :, e:e + 1], ALU.add)
        vector.tensor_copy(mn["km2"][:].rearrange("p (e j) -> p j e", e=8), kmv)
        for e in range(E):
            vector.tensor_tensor_scan(mn["sl2"][:, e * NT:(e + 1) * NT],
                                      mn["ones32"][:],
                                      mn["km2"][:, e * NT:(e + 1) * NT],
                                      0.0, ALU.mult, ALU.add)
        vector.tensor_copy(
            mn["totb"][:],
            mn["sl2"][:].rearrange("p (e j) -> p e j", j=NT)[:, :, NT - 1])
    c.V(v1)
    v1n = c.cV.n

    def pe1(tensor):
        tensor.wait_ge(c.cV.h, v1n)
        tensor.wait_ge(c.sG.h, sg_loads)
        c.cP.inc(tensor.matmul(pG[:, 0:8], c.strictU[:], mn["totb"][:],
                               start=True, stop=True))
    c.T(pe1)
    p1 = c.cP.n

    def a1(scalar):
        scalar.wait_ge(c.cP.h, p1)
        scalar.copy(mn["pfx"][:], pG[:, 0:8])
    c.A(a1)
    a1n = c.cA.n

    def v2(vector):
        vector.wait_ge(c.cA.h, a1n)
        for e in range(E):
            vector.tensor_scalar_add(mn["sl2"][:, e * NT:(e + 1) * NT],
                                     mn["sl2"][:, e * NT:(e + 1) * NT],
                                     mn["pfx"][:, e:e + 1])
        vector.tensor_copy(mn["km2"][:].rearrange("p (j e) -> p e j", j=NT),
                           mn["sl2"][:].rearrange("p (e j) -> p e j", j=NT))
        mv = mn["mine"][:].rearrange("p (j f) -> p j f", f=16)
        kmv = mn["km"][:].rearrange("p (j e) -> p j e", e=8)
        kcv = mn["kc"][:].rearrange("p (j e) -> p j e", e=8)
        slv = mn["km2"][:].rearrange("p (j e) -> p j e", e=8)
        i8 = c.iota8[:].unsqueeze(1).broadcast_to([128, NT, 8])
        selv = mn["sel"][:].rearrange("p (j e) -> p j e", e=8)
        tv = mn["t256"][:].rearrange("p (j e) -> p j e", e=8)
        for k in (1, 2):
            vector.tensor_single_scalar(selv, kcv, float(k), ALU.is_equal)
            vector.tensor_tensor(selv, selv, kmv, ALU.mult)
            vector.tensor_tensor(tv, selv, mv[:, :, 0:8], ALU.mult)
            vector.reduce_sum(mn["w%d" % k][:], tv, axis=AX.X)
            vector.tensor_scalar_mul(mn["w%d" % k][:], mn["w%d" % k][:], 0.5)
            vector.tensor_tensor(tv, selv, i8, ALU.mult)
            vector.reduce_sum(mn["ek"][:], tv, axis=AX.X)
            vector.tensor_tensor(tv, selv, slv, ALU.mult)
            vector.reduce_sum(mn["sk"][:], tv, axis=AX.X)
            vector.reduce_sum(mn["kk"][:], selv, axis=AX.X)
            vector.tensor_scalar(mn["off"][:], mn["ek"][:], float(PAD), -1.0,
                                 ALU.mult, ALU.add)
            vector.tensor_add(mn["off"][:], mn["off"][:], mn["sk"][:])
            vector.tensor_scalar(mn["t32"][:], mn["kk"][:], -OOB, OOB,
                                 ALU.mult, ALU.add)
            vector.tensor_add(mn["off"][:], mn["off"][:], mn["t32"][:])
            vector.tensor_single_scalar(mn["t32"][:], mn["sk"][:],
                                        float(PAD), ALU.is_gt)
            vector.tensor_scalar_mul(mn["t32"][:], mn["t32"][:], OOB)
            vector.tensor_add(mn["off"][:], mn["off"][:], mn["t32"][:])
            vector.tensor_single_scalar(mn["t32"][:], mn["sk"][:],
                                        float(PAD), ALU.is_le)
            vector.tensor_tensor(mn["w%d" % k][:], mn["w%d" % k][:],
                                 mn["t32"][:], ALU.mult)
            vector.tensor_copy(mn["offi%d" % k][:], mn["off"][:])
        vector.engine_nop()
    c.V(v2)


def emit_ffn_scatter(c, f, mn, xln_d, Xp_d, ywn=None, mine_cv=None, gz_n=None):
    scats = []
    for j in range(NT):
        par = j % 2

        stg = f["Xtm"] if ywn is None else f["Ybf"]

        def yl(sync, j=j, par=par, stg=stg):
            if len(scats) >= 2:
                sync.wait_ge(c.sG.h, scats[-2])
            if ywn is not None:
                sync.wait_ge(c.sY.h, ywn[j])
                if j < 2:
                    sync.wait_ge(c.sG.h, gz_n)
            c.sY.dma(sync.dma_start(
                out=stg[par][:],
                in_=xln_d.rearrange("(p j) d -> p j d", j=NT)[:, j, :]))
        c.Y(yl)
        ld = c.sY.n

        def g(gpsimd, j=j, par=par, ld=ld, stg=stg):
            gpsimd.wait_ge(c.sY.h, ld)
            if j == 0 and mine_cv is not None:
                gpsimd.wait_ge(c.cV.h, mine_cv)
            for k in (1, 2):
                c.sG.dma(gpsimd.indirect_dma_start(
                    Xp_d[:, :], IOA(mn["offi%d" % k][:, j:j + 1], 0),
                    stg[par][:], None,
                    bounds_check=c.bc_reg, oob_is_err=False))
        c.G(g)
        scats.append(c.sG.n)


def emit_ffn_experts(c, f, W1_d, W2_d, Xp_d, Yp_d, pA, pB, pC, pD, pE_, pF_,
                     w0r):
    w1_ready = [None] * E
    w2_ready = [None] * E
    mm1_done = [None] * E
    mm2_done = [None] * E
    cast_done = {}
    xbar_done = {}
    tr_done = {}
    h_ev = []
    yv = []
    w1_ready[0], w2_ready[0] = w0r
    scr = c.scr

    for e in range(E):
        for st in range(NST):
            par = st % 2

            def yl(sync, e=e, st=st, par=par):
                key = (e, st - 2)
                if key in cast_done:
                    sync.wait_ge(c.cV.h, cast_done[key])
                elif e > 0:
                    sync.wait_ge(c.cV.h, cast_done[(e - 1, st + NST - 2)])
                c.sY.dma(sync.dma_start(
                    out=f["Xtm"][par][:],
                    in_=Xp_d[e * PAD + st * 128: e * PAD + (st + 1) * 128, :]))
            c.Y(yl)
            ld = c.sY.n

            def vcst(vector, e=e, st=st, par=par, ld=ld):
                vector.wait_ge(c.sY.h, ld)
                key = (e, st - 1) if st > 0 else (e - 1, NST - 1)
                if key in tr_done:
                    vector.wait_ge(c.cP.h, tr_done[key])
                for ff in range(NF):
                    vector.tensor_copy(
                        scr[ff // 4][:, (ff % 4) * 128:(ff % 4) * 128 + 128],
                        f["Xtm"][par][:, ff * 128:(ff + 1) * 128])
            c.V(vcst)
            cast_done[(e, st)] = c.cV.n
            cn = c.cV.n

            def txb(tensor, e=e, st=st, cn=cn):
                tensor.wait_ge(c.cV.h, cn)
                key = (e, st - 1) if st > 0 else (e - 1, NST - 1)
                if key in xbar_done:
                    tensor.wait_ge(c.cV.h, xbar_done[key])
                for ff in range(NF):
                    pb = (pE_, pF_)[ff // 4]
                    c.cP.inc(tensor.transpose(
                        pb[:, (ff % 4) * 128:(ff % 4) * 128 + 128],
                        scr[ff // 4][:, (ff % 4) * 128:(ff % 4) * 128 + 128],
                        c.identf[:, 0:128]))
            c.T(txb)
            tr_done[(e, st)] = c.cP.n
            tn = c.cP.n

            def vxb(vector, e=e, st=st, tn=tn):
                vector.wait_ge(c.cP.h, tn)
                for ff in range(NF):
                    pb = (pE_, pF_)[ff // 4]
                    vector.tensor_copy(
                        f["Xfm"][:, ff * PAD + st * 128: ff * PAD + st * 128 + 128],
                        pb[:, (ff % 4) * 128:(ff % 4) * 128 + 128])
            c.V(vxb)
            xbar_done[(e, st)] = c.cV.n

        hb = f["H"][e % 2]
        for (c0, cw, ntl) in ((0, 512, 4), (512, PAD - 512, NST - 4)):
            for i in range(NI):
                pban = (pA, pB)[i % 2]

                def t1f(tensor, e=e, c0=c0, cw=cw, ntl=ntl, i=i, pban=pban):
                    if c0 == 0 and i == 0:
                        tensor.wait_ge(c.sG.h, w1_ready[e])
                    tensor.wait_ge(c.cV.h, xbar_done[(e, (c0 // 128) + ntl - 1)])
                    if len(h_ev) >= 2:
                        tensor.wait_ge(c.cV.h, h_ev[-2])
                    for ff in range(NF):
                        ins = tensor.matmul(
                            pban[:, 0:cw],
                            f["W1T"][:, ff * DI + i * 128: ff * DI + i * 128 + 128],
                            f["Xfm"][:, ff * PAD + c0: ff * PAD + c0 + cw],
                            start=(ff == 0), stop=(ff == NF - 1))
                    c.cP.inc(ins)
                c.T(t1f)
                mmd = c.cP.n

                def vr(vector, e=e, i=i, c0=c0, cw=cw, pban=pban, mmd=mmd, hb=hb):
                    vector.wait_ge(c.cP.h, mmd)
                    if e >= 2 and i == 0 and c0 == 0:
                        vector.wait_ge(c.cP.h, mm2_done[e - 2])
                    vector.tensor_relu(
                        hb[:, i * PAD + c0: i * PAD + c0 + cw], pban[:, 0:cw])
                c.V(vr)
                h_ev.append(c.cV.n)
        mm1_done[e] = c.cP.n
        h_all = c.cV.n

        if e + 1 < E:
            def gw1(gpsimd, e=e):
                gpsimd.wait_ge(c.cP.h, mm1_done[e])
                c.sG.dma(gpsimd.dma_start(
                    out=f["W1T"][:].rearrange("p (s i) -> p s i", s=NF),
                    in_=W1_d.rearrange("e (s p) i -> e p s i", p=128)[e + 1]))
            c.G(gw1)
            w1_ready[e + 1] = c.sG.n

        for st in range(NST):
            for half in range(2):
                pban = (pC, pD)[half]

                def t2f(tensor, e=e, st=st, half=half, pban=pban, h_all=h_all,
                        hb=hb):
                    if st == 0 and half == 0:
                        tensor.wait_ge(c.sG.h, w2_ready[e])
                        tensor.wait_ge(c.cV.h, h_all)
                    if len(yv) >= 2:
                        tensor.wait_ge(c.cV.h, yv[-2][0])
                    for i in range(NI):
                        ins = tensor.matmul(
                            pban[:, 0:512],
                            hb[:, i * PAD + st * 128: i * PAD + st * 128 + 128],
                            f["W2T"][:, i * D + half * 512: i * D + half * 512 + 512],
                            start=(i == 0), stop=(i == NI - 1))
                    c.cP.inc(ins)
                c.T(t2f)
                m2d = c.cP.n

                def vy(vector, st=st, half=half, pban=pban, m2d=m2d):
                    vector.wait_ge(c.cP.h, m2d)
                    if len(yv) >= 2:
                        vector.wait_ge(c.sY.h, yv[-2][1])
                    vector.tensor_copy(
                        f["Ybf"][st % 2][:, half * 512:(half + 1) * 512],
                        pban[:, 0:512])
                c.V(vy)
                ye = c.cV.n

                def ys(sync, e=e, st=st, half=half, ye=ye):
                    if half == 1:
                        sync.wait_ge(c.cV.h, ye)
                        c.sY.dma(sync.dma_start(
                            out=Yp_d[e * PAD + st * 128: e * PAD + (st + 1) * 128, :],
                            in_=f["Ybf"][st % 2][:]))
                c.Y(ys)
                yv.append((c.cV.n, c.sY.n))
        mm2_done[e] = c.cP.n

        if e + 1 < E:
            def gw2(gpsimd, e=e):
                gpsimd.wait_ge(c.cP.h, mm2_done[e])
                c.sG.dma(gpsimd.dma_start(
                    out=f["W2T"][:].rearrange("p (s o) -> p s o", s=NI),
                    in_=W2_d.rearrange("e (s p) o -> e p s o", p=128)[e + 1]))
            c.G(gw2)
            w2_ready[e + 1] = c.sG.n


def emit_gather_back(c, x_of, mn, g, Yp_d, pC, pD):
    vws = {}
    trs = {}
    for j in range(NT):
        par = j % 2

        def gg(gpsimd, j=j, par=par):
            if j >= 2:
                gpsimd.wait_ge(c.cV.h, vws[j - 2])
            for k in (1, 2):
                c.sG.dma(gpsimd.indirect_dma_start(
                    g["gb"][par][k - 1][:], None, Yp_d[:, :],
                    IOA(mn["offi%d" % k][:, j:j + 1], 0),
                    bounds_check=c.bc_reg, oob_is_err=False))
        c.G(gg)
        gd = c.sG.n

        def vw(vector, j=j, par=par, gd=gd):
            vector.wait_ge(c.sG.h, gd)
            if j >= 2:
                vector.wait_ge(c.cP.h, trs[j - 2])
            vector.tensor_scalar_mul(g["gw32"][:], g["gb"][par][0][:],
                                     mn["w1"][:, j:j + 1])
            vector.scalar_tensor_tensor(g["gwb"][par][:], g["gb"][par][1][:],
                                        mn["w2"][:, j:j + 1], g["gw32"][:],
                                        ALU.mult, ALU.add)
        c.V(vw)
        vws[j] = c.cV.n
        vn = c.cV.n

        def tgb(tensor, j=j, par=par, vn=vn):
            tensor.wait_ge(c.cV.h, vn)
            for ff in range(NF):
                pb = (pC, pD)[ff // 4]
                c.cP.inc(tensor.transpose(
                    pb[:, (ff % 4) * 128:(ff % 4) * 128 + 128],
                    g["gwb"][par][:, ff * 128:(ff + 1) * 128],
                    c.identf[:, 0:128]))
        c.T(tgb)
        trs[j] = c.cP.n
        tn = c.cP.n

        def va(vector, j=j, par=par, tn=tn):
            vector.wait_ge(c.cP.h, tn)
            for ff in range(NF):
                pb = (pC, pD)[ff // 4]
                xs = x_of(ff)
                vector.tensor_tensor(xs[:, j * 128:(j + 1) * 128],
                                     xs[:, j * 128:(j + 1) * 128],
                                     pb[:, (ff % 4) * 128:(ff % 4) * 128 + 128],
                                     ALU.add)
        c.V(va)
    barrier(c)


def emit_attn(c, x_of, rows, scr, xn, banks, ab, w, dr):
    pA, pB, pC, pD, pE_, pF_, pG, pH = banks

    def gl(gpsimd):
        c.sG.dma(gpsimd.dma_start(out=ab["whT"][:], in_=w["whT"][:, :]))
        c.sG.dma(gpsimd.dma_start(out=ab["blkT"][:], in_=w["blkT"][:, :]))
    c.G(gl)

    def yl(sync):
        c.sY.dma(sync.dma_start(out=ab["blk16"][:], in_=w["blk16"][:, :]))
        c.sY.dma(sync.dma_start(out=ab["bias"][:, 0:NF], in_=w["bq"][:, :]))
        c.sY.dma(sync.dma_start(out=ab["bias"][:, NF:2 * NF], in_=w["bk"][:, :]))
        c.sY.dma(sync.dma_start(out=ab["bias"][:, 2 * NF:3 * NF], in_=w["bv"][:, :]))
    c.Y(yl)
    barrier(c)

    for ch in range(NCH):
        xap = (lambda s, ch=ch: x_of(s)[:, ch * CH:(ch + 1) * CH])
        sm = emit_stats(c, xap, rows, scr, pG, pH)
        lcv = emit_ln_apply(c, xap, rows, xn, sm["cv"], pE_, pF_)

        def yx(sync, ch=ch, lcv=lcv):
            sync.wait_ge(c.cV.h, lcv)
            c.sY.dma(sync.dma_start(
                out=dr["xn2"][:, ch * NF * CH:(ch + 1) * NF * CH], in_=xn[:]))
        c.Y(yx)
        barrier(c)

    for qi, wkey, zkey, bof in ((0, "WqT", "zq", 0), (1, "WkT", "zk", 1)):
        def gw(gpsimd, wkey=wkey):
            c.sG.dma(gpsimd.dma_start(
                out=ab["Wbuf"][:].rearrange("p (s o) -> p s o", s=NF),
                in_=w[wkey].rearrange("(s p) o -> p s o", p=128)))
        c.G(gw)
        barrier(c)
        for ch in range(NCH):
            def yx2(sync, ch=ch):
                c.sY.dma(sync.dma_start(
                    out=xn[:], in_=dr["xn2"][:, ch * NF * CH:(ch + 1) * NF * CH]))
            c.Y(yx2)
            xl = c.sY.n
            aev = []
            for s in range(NF):
                pban = (pA, pB)[s % 2]

                def tq(tensor, s=s, pban=pban, xl=xl):
                    tensor.wait_ge(c.sY.h, xl)
                    if len(aev) >= 2:
                        tensor.wait_ge(c.cV.h, aev[-2])
                    for ff in range(NF):
                        ins = tensor.matmul(
                            pban[:, 0:CH],
                            ab["Wbuf"][:, ff * D + s * 128: ff * D + s * 128 + 128],
                            xn[:, ff * CH:(ff + 1) * CH],
                            start=(ff == 0), stop=(ff == NF - 1))
                    c.cP.inc(ins)
                c.T(tq)
                mq = c.cP.n

                def ae(vector, s=s, bof=bof, pban=pban, mq=mq):
                    vector.wait_ge(c.cP.h, mq)
                    vector.tensor_scalar_add(
                        ab["qk"][:, s * CH:(s + 1) * CH], pban[:, 0:CH],
                        ab["bias"][:, bof * NF + s: bof * NF + s + 1])
                c.V(ae)
                aev.append(c.cV.n)
            evd = c.cV.n

            def asq(scalar, ch=ch, qi=qi, evd=evd):
                scalar.wait_ge(c.cV.h, evd)
                for s in range(NF):
                    col = qi * NF * NCH + ch * NF + s
                    ins = scalar.activation(ab["jnk"][:],
                                            ab["qk"][:, s * CH:(s + 1) * CH],
                                            AF.Square,
                                            accum_out=ab["racc"][:, col:col + 1])
                ins
            c.A(asq)

            vz = []
            for h in range(HEADS):
                def tz(tensor, h=h, evd=evd):
                    tensor.wait_ge(c.cV.h, evd)
                    if vz:
                        tensor.wait_ge(c.cV.h, vz[-1])
                    base = (h % 2) * 64
                    c.cP.inc(tensor.matmul(
                        pC[0:64, 0:CH],
                        ab["whT"][base:base + 64, h * HD:(h + 1) * HD],
                        ab["qk"][base:base + 64,
                                 (h // 2) * CH:(h // 2) * CH + CH],
                        start=True, stop=True))
                c.T(tz)
                mz = c.cP.n

                def vzf(vector, h=h, mz=mz):
                    vector.wait_ge(c.cP.h, mz)
                    base = (h % 2) * 64
                    vector.tensor_copy(
                        ab["zbuf"][base:base + 64, (h // 2) * CH:(h // 2) * CH + CH],
                        pC[0:64, 0:CH])
                c.V(vzf)
                vz.append(c.cV.n)
            zn = c.cV.n

            def yz(sync, ch=ch, zkey=zkey, zn=zn):
                sync.wait_ge(c.cV.h, zn)
                c.sY.dma(sync.dma_start(
                    out=dr[zkey][:, ch * NF * CH:(ch + 1) * NF * CH],
                    in_=ab["zbuf"][:]))
            c.Y(yz)
            barrier(c)

    def vn(vector):
        vector.reduce_sum(c.statcol[:, 0:1],
                          ab["racc"][:].rearrange("p (q x) -> p q x", q=2)[:, 0, :],
                          axis=AX.X)
        vector.reduce_sum(
            c.statcol[:, 1:2],
            ab["racc"][:].rearrange("p (q x) -> p q x", q=2)[:, 1, :], axis=AX.X)
    c.V(vn)
    v1 = c.cV.n

    def tn(tensor):
        tensor.wait_ge(c.cV.h, v1)
        c.cP.inc(tensor.matmul(pG[0:1, 0:2], c.ones_col[:, 0:1],
                               c.statcol[:, 0:2], start=True, stop=True))
    c.T(tn)
    p1 = c.cP.n

    def an(scalar):
        scalar.wait_ge(c.cP.h, p1)
        scalar.copy(ab["nrm"][0:1, 0:2], pG[0:1, 0:2])
        scalar.activation(ab["nrm"][0:1, 2:4], ab["nrm"][0:1, 0:2], AF.Ln,
                          scale=8.0)
        scalar.activation(ab["nrm"][0:1, 4:6], ab["nrm"][0:1, 2:4],
                                   AF.Exp, scale=-0.5)
    c.A(an)
    a1 = c.cA.n

    def vc(vector):
        vector.wait_ge(c.cA.h, a1)
        vector.tensor_tensor(ab["nrm"][0:1, 6:8], ab["nrm"][0:1, 4:6],
                             ab["nrm"][0:1, 4:6], ALU.mult)
        vector.tensor_tensor(ab["nrm"][0:1, 6:8], ab["nrm"][0:1, 6:8],
                             ab["nrm"][0:1, 0:2], ALU.mult)
        vector.tensor_scalar_mul(ab["nrm"][0:1, 6:8], ab["nrm"][0:1, 6:8], 8.0)
        vector.tensor_add(ab["nrm"][0:1, 2:3], ab["nrm"][0:1, 6:7],
                                   ab["nrm"][0:1, 7:8])
    c.V(vc)
    v2 = c.cV.n

    def ac(scalar):
        scalar.wait_ge(c.cV.h, v2)
        scalar.activation(ab["nrm"][0:1, 3:4], ab["nrm"][0:1, 2:3],
                                   AF.Exp, scale=-0.5)
    c.A(ac)
    a2 = c.cA.n

    def vc2(vector):
        vector.wait_ge(c.cA.h, a2)
        vector.tensor_scalar_mul(ab["nrm"][0:1, 3:4],
                                          ab["nrm"][0:1, 3:4], 1.0 / 64.0)
    c.V(vc2)
    v3 = c.cV.n

    def tb(tensor):
        tensor.wait_ge(c.cV.h, v3)
        c.cP.inc(tensor.matmul(pG[:, 16:18], c.ones_row[0:1, :],
                               ab["nrm"][0:1, 4:6], start=True, stop=True))
        c.cP.inc(tensor.matmul(pH[0:16, 0:1], c.ones_row[0:1, 0:16],
                               ab["nrm"][0:1, 3:4], start=True, stop=True))
    c.T(tb)
    p2 = c.cP.n

    def ab2(scalar):
        scalar.wait_ge(c.cP.h, p2)
        scalar.copy(ab["rcol"][:, 0:2], pG[:, 16:18])
        scalar.copy(ab["ccol"][0:16, 0:1], pH[0:16, 0:1])
    c.A(ab2)
    barrier(c)

    def gwv(gpsimd):
        c.sG.dma(gpsimd.dma_start(
            out=ab["Wbuf"][:].rearrange("p (s o) -> p s o", s=NF),
            in_=w["WvT"].rearrange("(s p) o -> p s o", p=128)))
    c.G(gwv)
    barrier(c)

    for ch in range(NCH):
        def yb(sync, ch=ch):
            c.sY.dma(sync.dma_start(
                out=xn[:], in_=dr["xn2"][:, ch * NF * CH:(ch + 1) * NF * CH]))
            c.sY.dma(sync.dma_start(
                out=ab["qk"][:], in_=dr["zq"][:, ch * NF * CH:(ch + 1) * NF * CH]))
            c.sY.dma(sync.dma_start(
                out=ab["zbuf"][:], in_=dr["zk"][:, ch * NF * CH:(ch + 1) * NF * CH]))
        c.Y(yb)
        xl = c.sY.n

        vev = []
        for s in range(NF):
            pban = (pA, pB)[s % 2]

            def tv(tensor, s=s, pban=pban, xl=xl):
                tensor.wait_ge(c.sY.h, xl)
                if len(vev) >= 2:
                    tensor.wait_ge(c.cV.h, vev[-2])
                for ff in range(NF):
                    ins = tensor.matmul(
                        pban[:, 0:CH],
                        ab["Wbuf"][:, ff * D + s * 128: ff * D + s * 128 + 128],
                        xn[:, ff * CH:(ff + 1) * CH],
                        start=(ff == 0), stop=(ff == NF - 1))
                c.cP.inc(ins)
            c.T(tv)
            mv = c.cP.n

            def av(vector, s=s, pban=pban, mv=mv):
                vector.wait_ge(c.cP.h, mv)
                vector.tensor_scalar_add(
                    ab["vbuf"][:, s * CH:(s + 1) * CH], pban[:, 0:CH],
                    ab["bias"][:, 2 * NF + s: 2 * NF + s + 1])
            c.V(av)
            vev.append(c.cV.n)
        vvd = c.cV.n

        def vu(vector, xl=xl):
            vector.wait_ge(c.sY.h, xl)
            vector.wait_ge(c.cV.h, vvd)
            for s in range(NF):
                vector.tensor_scalar_mul(c.t1[:], ab["qk"][:, s * CH:(s + 1) * CH],
                                         ab["rcol"][:, 0:1])
                vector.scalar_tensor_tensor(scr[0][:],
                                            ab["zbuf"][:, s * CH:(s + 1) * CH],
                                            ab["rcol"][:, 1:2], c.t1[:],
                                            ALU.mult, ALU.add)
                vector.tensor_copy(xn[:, s * CH:(s + 1) * CH], scr[0][:])
            vector.engine_nop()
        c.V(vu)
        vun = c.cV.n

        def aexp(scalar, vun=vun):
            scalar.wait_ge(c.cV.h, vun)
            for s in range(NF):
                ins = scalar.activation(ab["qk"][:, s * CH:(s + 1) * CH],
                                        xn[:, s * CH:(s + 1) * CH], AF.Exp)
            ins
        c.A(aexp)
        axn = c.cA.n

        def ts(tensor, axn=axn):
            tensor.wait_ge(c.cA.h, axn)
            for s in range(NF):
                ins = tensor.matmul(pC[0:16, 0:CH],
                                    ab["blkT"][:, s * 16:(s + 1) * 16],
                                    ab["qk"][:, s * CH:(s + 1) * CH],
                                    start=(s == 0), stop=(s == NF - 1))
            c.cP.inc(ins)
        c.T(ts)
        tsn = c.cP.n

        def vs(vector, tsn=tsn):
            vector.wait_ge(c.cP.h, tsn)
            vector.tensor_copy(ab["sbuf"][:], pC[0:16, 0:CH])
            vector.tensor_scalar_mul(ab["sc"][:], ab["sbuf"][:], ab["ccol"][0:16, 0:1])
            vector.reciprocal(ab["ratio"][:], ab["sc"][:])
            vector.tensor_tensor(ab["ratio"][:], ab["ratio"][:],
                                          ab["sc"][:], ALU.mult)
        c.V(vs)
        vsn = c.cV.n

        vo = []
        for s in range(NF):
            def tr(tensor, s=s, vsn=vsn):
                tensor.wait_ge(c.cV.h, vsn)
                if vo:
                    tensor.wait_ge(c.cV.h, vo[-1])
                c.cP.inc(tensor.matmul(pE_[:, 0:CH],
                                       ab["blk16"][:, s * 128:(s + 1) * 128],
                                       ab["ratio"][:], start=True, stop=True))
            c.T(tr)
            trn = c.cP.n

            def vof(vector, s=s, ch=ch, trn=trn, vvd=vvd):
                vector.wait_ge(c.cP.h, trn)
                vector.wait_ge(c.cV.h, vvd)
                vector.tensor_tensor(c.t1[:], pE_[:, 0:CH],
                                     ab["vbuf"][:, s * CH:(s + 1) * CH], ALU.mult)
                vector.tensor_tensor(
                    x_of(s)[:, ch * CH:(ch + 1) * CH],
                    x_of(s)[:, ch * CH:(ch + 1) * CH], c.t1[:], ALU.add)
            c.V(vof)
            vo.append(c.cV.n)
        barrier(c)



def emit_logits_phase(c, x_src, rows, scr, lgt_local, pG, pH, pI, pJ,
                      xch=None, x_d=None):
    hist = {}
    for ch in range(NCH):
        par = ch % 2
        if x_src is None:
            def ldx(sync, ch=ch, par=par):
                if ch >= 2:
                    w = hist[ch - 2]
                    sync.wait_ge(c.cP.h, w["lg_pe"])
                    sync.wait_ge(c.cA.h, w["sq"])
                c.sY.dma(sync.dma_start(
                    out=xch[par][:].rearrange("p (s t) -> p s t", s=NF),
                    in_=x_d.rearrange("(s p) t -> p s t",
                                      p=128)[:, :, ch * CH:(ch + 1) * CH]))
            c.Y(ldx)
            ldm = c.sY.n
            xap = (lambda s, par=par: xch[par][:, s * CH:(s + 1) * CH])
            pre = [(c.sY, ldm)]
        else:
            xap = (lambda s, ch=ch: x_src(s, ch))
            pre = []
        if ch >= 1:
            w = hist[ch - 1]
            pre = pre + [(c.cV, w["cv"]), (c.cP, w["lg_pe"]),
                         (c.cA, w["evac"]), (c.cV, w["lg_cv"])]
        sm = emit_stats(c, xap, rows, scr, pG, pH, pre_waits=pre)
        lm = emit_logits(c, xap, ch, rows, sm["cv"], lgt_local, pH, pI, pJ)
        hist[ch] = dict(cv=sm["cv"], sq=sm["sq"], evac=sm["evac"],
                        stats_pe=sm["stats_pe"], lg_pe=lm["pe"], lg_cv=lm["cv"])
    return hist


def build_L1():
    nc = bass.Bass("TRN2", target_bir_lowering=False)
    x_d = nc.dram_tensor("x_fm", [D, TC], F32, kind="ExternalInput").ap()
    eeT_d = nc.dram_tensor("eeT", [D, E], F32, kind="ExternalInput").ap()
    ident_d = nc.dram_tensor("ident", [128, 128], F32, kind="ExternalInput").ap()
    lg_out = nc.dram_tensor("lg_out", [TC, E], F32, kind="ExternalOutput").ap()

    with ExitStack() as st:
        en = st.enter_context
        block = en(nc.Block())
        c = mk_ctx(nc, st, block)
        xch = [en(nc.sbuf_tensor("xch%d" % i, [128, NF * CH], F32))
               for i in range(2)]
        eeT = en(nc.sbuf_tensor("eeT_s", [128, NF * 8], F32))
        identf = en(nc.sbuf_tensor("identf", [128, 128], F32))
        ones_col = en(nc.sbuf_tensor("ones_col", [128, 1], F32))
        ones_row = en(nc.sbuf_tensor("ones_row", [1, 128], F32))
        negs8 = en(nc.sbuf_tensor("negs8", [1, 8], F32))
        rows_t = en(nc.sbuf_tensor("rows", [1, 5 * CH], F32))
        scr = [en(nc.sbuf_tensor("scr%d" % i, [128, CH], F32)) for i in range(2)]
        rows = {n: rows_t[0:1, i * CH:(i + 1) * CH] for i, n in enumerate(
            ["sum", "sumsq", "mean", "rstd", "negmrstd"])}
        rows["tmp"] = scr[0][0:1, 0:CH]
        rows["tmp2"] = scr[1][0:1, 0:CH]
        lgt_local = en(nc.sbuf_tensor("lgt_local", [128, NT * 8], F32))
        t1 = en(nc.sbuf_tensor("t1", [128, CH], F32))
        epsb = en(nc.sbuf_tensor("epsb", [1, 1], F32))
        pG = en(nc.psum_tensor("pG", [128, 512], F32))
        pH = en(nc.psum_tensor("pH", [128, 512], F32))
        pI = en(nc.psum_tensor("pI", [128, 512], F32))
        pJ = en(nc.psum_tensor("pJ", [128, 512], F32))
        c.ones_col = ones_col[:]
        c.ones_row = ones_row
        c.eeT = eeT
        c.negs8 = negs8
        c.identf = identf
        c.t1 = t1
        c.epsb = epsb[0:1, 0:1]

        def loads(sync):
            c.sY.dma(sync.dma_start(
                out=eeT[:].rearrange("p (s e) -> p s e", s=NF),
                in_=eeT_d.rearrange("(s p) e -> p s e", p=128)))
            c.sY.dma(sync.dma_start(out=identf[:], in_=ident_d[:, :]))
        c.Y(loads)

        def vinit(vector):
            vector.memset(ones_col[:], 1.0)
            vector.memset(epsb[:], EPS)
            vector.memset(ones_row[:], 1.0)
        c.V(vinit)
        barrier(c)

        def pe_se(tensor):
            for s in range(NF):
                ins = tensor.matmul(pG[0:1, 0:8], ones_col[:, 0:1],
                                    eeT[:, s * 8:(s + 1) * 8],
                                    start=(s == 0), stop=(s == NF - 1))
            c.cP.inc(ins)
        c.T(pe_se)
        p1 = c.cP.n

        def v_negs(vector):
            vector.wait_ge(c.cP.h, p1)
            vector.tensor_scalar_mul(negs8[0:1, :], pG[0:1, 0:8], -1.0)
        c.V(v_negs)
        barrier(c)

        emit_logits_phase(c, None, rows, scr, lgt_local, pG, pH, pI, pJ,
                          xch=xch, x_d=x_d)
        lgd = c.cV.n

        def store(sync):
            sync.wait_ge(c.cV.h, lgd)
            c.sY.dma(sync.dma_start(
                out=lg_out.rearrange("(j r) e -> r j e", r=128),
                in_=lgt_local[:].rearrange("p (j e) -> p j e", e=8)))
            sync.wait_ge(c.sY.h, c.sY.n)
        c.Y(store)
    return nc


def build_L23(with_attn):
    nc = bass.Bass("TRN2", target_bir_lowering=False)
    x_d = nc.dram_tensor("x_fm", [D, TC], F32, kind="ExternalInput").ap()
    lgt_d = nc.dram_tensor("lgt_full", [TOK, E], F32, kind="ExternalInput").ap()
    W1_d = nc.dram_tensor("W1T", [E, D, DI], F32, kind="ExternalInput").ap()
    W2_d = nc.dram_tensor("W2T", [E, DI, D], F32, kind="ExternalInput").ap()
    ident_d = nc.dram_tensor("ident", [128, 128], F32, kind="ExternalInput").ap()
    triU_d = nc.dram_tensor("strictU", [128, 128], F32, kind="ExternalInput").ap()
    iota8_d = nc.dram_tensor("iota8", [128, 8], F32, kind="ExternalInput").ap()
    gidx_d = nc.dram_tensor("gidx", [128, NT], I32, kind="ExternalInput").ap()
    if with_attn:
        WqT_d = nc.dram_tensor("WqT", [D, D], F32, kind="ExternalInput").ap()
        WkT_d = nc.dram_tensor("WkT", [D, D], F32, kind="ExternalInput").ap()
        WvT_d = nc.dram_tensor("WvT", [D, D], F32, kind="ExternalInput").ap()
        bq_d = nc.dram_tensor("bq", [128, NF], F32, kind="ExternalInput").ap()
        bk_d = nc.dram_tensor("bk", [128, NF], F32, kind="ExternalInput").ap()
        bv_d = nc.dram_tensor("bv", [128, NF], F32, kind="ExternalInput").ap()
        whT_d = nc.dram_tensor("whT", [128, HEADS * HD], F32, kind="ExternalInput").ap()
        blkT_d = nc.dram_tensor("blkT", [128, NF * HEADS], F32, kind="ExternalInput").ap()
        blk16_d = nc.dram_tensor("blk16", [HEADS, D], F32, kind="ExternalInput").ap()
        eeT_d = nc.dram_tensor("eeT", [D, E], F32, kind="ExternalInput").ap()
        x2_out = nc.dram_tensor("x2_out", [D, TC], F32, kind="ExternalOutput").ap()
        lg_out = nc.dram_tensor("lg_out", [TC, E], F32, kind="ExternalOutput").ap()
    else:
        y_out = nc.dram_tensor("y_out", [D, TC], F32, kind="ExternalOutput").ap()

    xln_dr = nc.dram_tensor("xln_d", [TC, D], BF16, kind="Internal").ap()
    t16_dr = nc.dram_tensor("t16_d", [TOK, 16], F32, kind="Internal").ap()
    Xp_dr = nc.dram_tensor("Xpad_d", [E * PAD, D], BF16, kind="Internal").ap()
    Yp_dr = nc.dram_tensor("Ypad_d", [E * PAD, D], BF16, kind="Internal").ap()
    if with_attn:
        zq_dr = nc.dram_tensor("zq_d", [128, NF * TC], BF16, kind="Internal").ap()
        zk_dr = nc.dram_tensor("zk_d", [128, NF * TC], BF16, kind="Internal").ap()
        xn2_dr = nc.dram_tensor("xn2_d", [128, NF * TC], BF16, kind="Internal").ap()

    with ExitStack() as st:
        en = st.enter_context
        block = en(nc.Block())
        c = mk_ctx(nc, st, block)

        identf = en(nc.sbuf_tensor("identf", [128, 128], F32))
        strictU = en(nc.sbuf_tensor("strictU_s", [128, 128], BF16))
        iota8 = en(nc.sbuf_tensor("iota8_s", [128, 8], F32))
        gidx = en(nc.sbuf_tensor("gidx_s", [128, NT], I32))
        ones_col = en(nc.sbuf_tensor("ones_col", [128, 1], F32))
        ones_row = en(nc.sbuf_tensor("ones_row", [1, 128], F32))
        rows_t = en(nc.sbuf_tensor("rows", [1, 5 * CH], F32))
        scr = [en(nc.sbuf_tensor("scr%d" % i, [128, CH], F32)) for i in range(2)]
        rows = {n: rows_t[0:1, i * CH:(i + 1) * CH] for i, n in enumerate(
            ["sum", "sumsq", "mean", "rstd", "negmrstd"])}
        rows["tmp"] = scr[0][0:1, 0:CH]
        rows["tmp2"] = scr[1][0:1, 0:CH]
        statcol = en(nc.sbuf_tensor("statcol", [128, 2], F32))
        t1 = en(nc.sbuf_tensor("t1", [128, CH], F32))
        xn = en(nc.sbuf_tensor("xn", [128, NF * CH], BF16))
        epsb = en(nc.sbuf_tensor("epsb", [1, 1], F32))
        c.ones_col = ones_col[:]
        c.ones_row = ones_row
        c.identf = identf
        c.strictU = strictU[:]
        c.iota8 = iota8
        c.gidx = gidx
        c.statcol = statcol
        c.t1 = t1
        c.scr = scr
        c.epsb = epsb[0:1, 0:1]

        pA = en(nc.psum_tensor("pA", [128, 512], F32))
        pB = en(nc.psum_tensor("pB", [128, 512], F32))
        pC = en(nc.psum_tensor("pC", [128, 512], F32))
        pD = en(nc.psum_tensor("pD", [128, 512], F32))
        pE_ = en(nc.psum_tensor("pE", [128, 512], F32))
        pF_ = en(nc.psum_tensor("pF", [128, 512], F32))
        pG = en(nc.psum_tensor("pG", [128, 512], F32))
        pH = en(nc.psum_tensor("pH", [128, 512], F32))

        def g_loads(gpsimd):
            c.bc_reg = gpsimd.alloc_register("bc")
            gpsimd.reg_mov(c.bc_reg, E * PAD - 1)
            c.sG.dma(gpsimd.dma_start(out=gidx[:], in_=gidx_d[:, :]))
            c.sG.dma(gpsimd.dma_start(out=strictU[:], in_=triU_d[:, :]))
        c.G(g_loads)
        sg_loads = c.sG.n

        def loads(sync):
            c.sY.dma(sync.dma_start(out=identf[:], in_=ident_d[:, :]))
            c.sY.dma(sync.dma_start(out=iota8[:], in_=iota8_d[:, :]))
        c.Y(loads)

        def vinit(vector):
            vector.memset(ones_col[:], 1.0)
            vector.memset(epsb[:], EPS)
            vector.memset(ones_row[:], 1.0)
        c.V(vinit)
        barrier(c)

        mn = {}
        with ExitStack() as stm:
            enm = stm.enter_context
            mn["mine"] = enm(nc.sbuf_tensor("mine", [128, NT * 16], F32))
            mn["km"] = enm(nc.sbuf_tensor("km", [128, NT * 8], F32))
            mn["kc"] = enm(nc.sbuf_tensor("kc", [128, NT * 8], F32))
            mn["km2"] = enm(nc.sbuf_tensor("km2", [128, NT * 8], F32))
            mn["sl2"] = enm(nc.sbuf_tensor("sl2", [128, NT * 8], F32))
            mn["sel"] = enm(nc.sbuf_tensor("sel", [128, NT * 8], F32))
            mn["t256"] = enm(nc.sbuf_tensor("t256", [128, NT * 8], F32))
            mn["totb"] = enm(nc.sbuf_tensor("mtotb", [128, 8], BF16))
            mn["pfx"] = enm(nc.sbuf_tensor("mpfx", [128, 8], F32))
            mn["ones32"] = enm(nc.sbuf_tensor("ones32", [128, NT], F32))
            for nm in ["w1", "w2", "ek", "sk", "kk", "off", "t32"]:
                mn[nm] = enm(nc.sbuf_tensor("m_" + nm, [128, NT], F32))
            mn["offi1"] = enm(nc.sbuf_tensor("offi1", [128, NT], I32))
            mn["offi2"] = enm(nc.sbuf_tensor("offi2", [128, NT], I32))

            with ExitStack() as str_:
                enr = str_.enter_context
                rt = {}
                rt["lgt"] = enr(nc.sbuf_tensor("lgt", [128, 2048], F32))
                rt["l2"] = enr(nc.sbuf_tensor("rt_l2", [128, 2048], F32))
                rt["w"] = enr(nc.sbuf_tensor("rt_w", [128, 2048], F32))
                rt["msk2"] = enr(nc.sbuf_tensor("rt_msk2", [128, 2048], F32))
                rt["cs2"] = enr(nc.sbuf_tensor("rt_cs2", [128, 2048], F32))
                rt["t2048"] = enr(nc.sbuf_tensor("rt_t2048", [128, 2048], F32))
                for nm in ["m1", "m2", "d21", "e21", "w1", "w2", "tmp", "ones256"]:
                    rt[nm] = enr(nc.sbuf_tensor("rt_" + nm, [128, 256], F32))
                rt["totb"] = enr(nc.sbuf_tensor("rt_totb", [128, 8], BF16))
                rt["pfx"] = enr(nc.sbuf_tensor("rt_pfx", [128, 8], F32))

                def lgl(sync):
                    c.sY.dma(sync.dma_start(
                        out=rt["lgt"][:],
                        in_=lgt_d.rearrange("(p j) e -> p (j e)", p=128)))
                c.Y(lgl)
                barrier(c)
                t16_done = emit_routing(c, rt, pG, t16_dr)
                emit_mine(c, mn, pG, t16_dr, t16_done, sg_loads)
                mine_cv = c.cV.n
                mine_ca = c.cA.n

            with ExitStack() as stf:
                enf = stf.enter_context
                f = {}
                f["W1T"] = enf(nc.sbuf_tensor("W1T_s", [128, NF * DI], BF16))
                f["W2T"] = enf(nc.sbuf_tensor("W2T_s", [128, NI * D], BF16))
                f["Xfm"] = enf(nc.sbuf_tensor("Xfm", [128, NF * PAD], BF16))
                f["H"] = [enf(nc.sbuf_tensor("H%d" % i, [128, NI * PAD], BF16))
                          for i in range(2)]
                f["Xtm"] = [enf(nc.sbuf_tensor("fXtm%d" % i, [128, D], BF16))
                            for i in range(4)]
                f["Ybf"] = [enf(nc.sbuf_tensor("Ybf%d" % i, [128, D], BF16))
                            for i in range(2)]
                xch = [enf(nc.sbuf_tensor("xch%d" % i, [128, NF * CH], F32))
                       for i in range(2)]

                def gw0(gpsimd):
                    c.sG.dma(gpsimd.dma_start(
                        out=f["W1T"][:].rearrange("p (s i) -> p s i", s=NF),
                        in_=W1_d.rearrange("e (s p) i -> e p s i", p=128)[0]))
                    c.sG.dma(gpsimd.dma_start(
                        out=f["W2T"][:].rearrange("p (s o) -> p s o", s=NI),
                        in_=W2_d.rearrange("e (s p) o -> e p s o", p=128)[0]))
                c.G(gw0)
                w0r = (c.sG.n - 16, c.sG.n)

                def vz(vector):
                    vector.memset(f["Ybf"][0][:], 0.0)
                c.V(vz)
                vzn = c.cV.n

                def gz(gpsimd):
                    gpsimd.wait_ge(c.cV.h, vzn)
                    for b in range(E * PAD // 128):
                        c.sG.dma(gpsimd.dma_start(
                            out=Xp_dr[b * 128:(b + 1) * 128, :],
                            in_=f["Ybf"][0][:]))
                c.G(gz)
                gz_n = c.sG.n

                hist = {}
                xcop = []
                screl = []
                for ch in range(NCH):
                    par = ch % 2

                    def ldx(sync, ch=ch, par=par):
                        if ch >= 2:
                            w = hist[ch - 2]
                            sync.wait_ge(c.cP.h, w["ln_pe"])
                            sync.wait_ge(c.cA.h, w["sq"])
                            sync.wait_ge(c.cV.h, w["ln_cv"])
                        c.sY.dma(sync.dma_start(
                            out=xch[par][:].rearrange("p (s t) -> p s t", s=NF),
                            in_=x_d.rearrange("(s p) t -> p s t",
                                              p=128)[:, :, ch * CH:(ch + 1) * CH]))
                    c.Y(ldx)
                    ldm = c.sY.n
                    xap = (lambda s, par=par: xch[par][:, s * CH:(s + 1) * CH])
                    pre = [(c.sY, ldm)]
                    if ch == 0:
                        pre = pre + [(c.cA, mine_ca)]
                    if ch >= 1:
                        w = hist[ch - 1]
                        pre = pre + [(c.cV, w["cv"]), (c.cA, w["evac"]),
                                     (c.cP, w["ln_pe"])]
                    sm = emit_stats(c, xap, rows, scr, pG, pH, pre_waits=pre)
                    pe_pre = []
                    if ch >= 1:
                        pe_pre = [(c.cV, hist[ch - 1]["ln_cv"])]
                    lcv = emit_ln_apply(c, xap, rows, xch[par][:], sm["cv"],
                                        pE_, pF_, pe_pre=pe_pre,
                                        v_pre=[(c.cA, sm["sq"])])
                    for jt in range(4):
                        tile = ch * 4 + jt
                        tpar = jt

                        def txb(tensor, ch=ch, par=par, jt=jt, lcv=lcv):
                            tensor.wait_ge(c.cV.h, lcv)
                            if xcop:
                                tensor.wait_ge(c.cV.h, xcop[-1])
                            for ff in range(NF):
                                pb = (pC, pD)[ff // 4]
                                c.cP.inc(tensor.transpose(
                                    pb[:, (ff % 4) * 128:(ff % 4) * 128 + 128],
                                    xch[par][:, ff * CH + jt * 128:
                                             ff * CH + jt * 128 + 128],
                                    identf[:, 0:128]))
                        c.T(txb)
                        tn = c.cP.n

                        def vxb(vector, tile=tile, tpar=tpar, tn=tn):
                            vector.wait_ge(c.cP.h, tn)
                            if tile >= 4:
                                vector.wait_ge(c.sA.h, (tile - 4) // 4 + 1)
                            for ff in range(NF):
                                pb = (pC, pD)[ff // 4]
                                vector.tensor_copy(
                                    f["Xtm"][tpar][:, ff * 128:(ff + 1) * 128],
                                    pb[:, (ff % 4) * 128:(ff % 4) * 128 + 128])
                        c.V(vxb)
                        xcop.append(c.cV.n)
                        cn = c.cV.n

                        def g(gpsimd, tile=tile, jt=jt, tpar=tpar, cn=cn):
                            gpsimd.wait_ge(c.cV.h, cn)
                            if tile == 0:
                                gpsimd.wait_ge(c.cV.h, mine_cv)
                                gpsimd.wait_ge(c.sG.h, gz_n)
                            for k in (1, 2):
                                c.sG.dma(gpsimd.indirect_dma_start(
                                    Xp_dr[:, :],
                                    IOA(mn["offi%d" % k][:, tile:tile + 1], 0),
                                    f["Xtm"][tpar][:], None,
                                    bounds_check=c.bc_reg, oob_is_err=False))
                            if jt == 3:
                                gpsimd.wait_ge(c.sG.h, c.sG.n)
                                c.sA.inc(gpsimd.engine_nop(), 1)
                        c.G(g)
                    hist[ch] = dict(cv=sm["cv"], sq=sm["sq"], evac=sm["evac"],
                                    ln_pe=c.cP.n, ln_cv=lcv)
                barrier(c)
                emit_ffn_experts(c, f, W1_d, W2_d, Xp_dr, Yp_dr,
                                 pA, pB, pC, pD, pE_, pF_, w0r)

            x2_sb = en(nc.sbuf_tensor("x2_sb", [128, NF * TC], F32, side="right"))

            def x_of(s):
                return x2_sb[:, s * TC:(s + 1) * TC]

            def xload(sync):
                c.sY.dma(sync.dma_start(
                    out=x2_sb[:].rearrange("p (s t) -> p s t", s=NF),
                    in_=x_d.rearrange("(s p) t -> p s t", p=128)))
            c.Y(xload)

            with ExitStack() as stg:
                eng = stg.enter_context
                g = {}
                g["gb"] = [[eng(nc.sbuf_tensor("gb%d_%d" % (pp, i), [128, D], BF16))
                            for i in range(2)] for pp in range(2)]
                g["gw32"] = eng(nc.sbuf_tensor("gw32", [128, D], F32))
                g["gwb"] = [eng(nc.sbuf_tensor("gwb%d" % pp, [128, D], F32))
                            for pp in range(2)]
                barrier(c)
                emit_gather_back(c, x_of, mn, g, Yp_dr, pC, pD)

        if with_attn:
            with ExitStack() as sta:
                ena = sta.enter_context
                ab = {}
                ab["Wbuf"] = ena(nc.sbuf_tensor("aWbuf", [128, NF * D], BF16))
                ab["qk"] = ena(nc.sbuf_tensor("aqk", [128, NF * CH], BF16))
                ab["zbuf"] = ena(nc.sbuf_tensor("azbuf", [128, NF * CH], BF16))
                ab["vbuf"] = ena(nc.sbuf_tensor("avbuf", [128, NF * CH], BF16))
                ab["whT"] = ena(nc.sbuf_tensor("awhT", [128, HEADS * HD], BF16))
                ab["bias"] = ena(nc.sbuf_tensor("abias", [128, 3 * NF], F32))
                ab["racc"] = ena(nc.sbuf_tensor("aracc", [128, 2 * NF * NCH], F32))
                ab["jnk"] = ena(nc.sbuf_tensor("ajnk", [128, CH], BF16))
                ab["nrm"] = ena(nc.sbuf_tensor("anrm", [1, 8], F32))
                ab["rcol"] = ena(nc.sbuf_tensor("arcol", [128, 2], F32))
                ab["blkT"] = ena(nc.sbuf_tensor("ablkT", [128, NF * HEADS], BF16))
                ab["blk16"] = ena(nc.sbuf_tensor("ablk16", [16, D], F32))
                ab["sbuf"] = ena(nc.sbuf_tensor("asbuf", [16, CH], F32))
                ab["sc"] = ena(nc.sbuf_tensor("asc", [16, CH], F32))
                ab["ratio"] = ena(nc.sbuf_tensor("aratio", [16, CH], F32))
                ab["ccol"] = ena(nc.sbuf_tensor("accol", [16, 1], F32))
                w = dict(WqT=WqT_d, WkT=WkT_d, WvT=WvT_d, bq=bq_d, bk=bk_d,
                         bv=bv_d, whT=whT_d, blkT=blkT_d, blk16=blk16_d)
                dr = dict(zq=zq_dr, zk=zk_dr, xn2=xn2_dr)
                emit_attn(c, x_of, rows, scr, xn,
                          (pA, pB, pC, pD, pE_, pF_, pG, pH), ab, w, dr)

            eeT = en(nc.sbuf_tensor("eeT_s", [128, NF * 8], F32))
            negs8 = en(nc.sbuf_tensor("negs8", [1, 8], F32))
            lgt_local = en(nc.sbuf_tensor("lgt_local", [128, NT * 8], F32))
            c.eeT = eeT
            c.negs8 = negs8

            def eel(sync):
                c.sY.dma(sync.dma_start(
                    out=eeT[:].rearrange("p (s e) -> p s e", s=NF),
                    in_=eeT_d.rearrange("(s p) e -> p s e", p=128)))
            c.Y(eel)
            barrier(c)

            def pe_se(tensor):
                for s in range(NF):
                    ins = tensor.matmul(pG[0:1, 0:8], ones_col[:, 0:1],
                                        eeT[:, s * 8:(s + 1) * 8],
                                        start=(s == 0), stop=(s == NF - 1))
                c.cP.inc(ins)
            c.T(pe_se)
            p1 = c.cP.n

            def v_negs(vector):
                vector.wait_ge(c.cP.h, p1)
                vector.tensor_scalar_mul(negs8[0:1, :], pG[0:1, 0:8], -1.0)
            c.V(v_negs)
            barrier(c)

            emit_logits_phase(
                c, lambda s, ch: x2_sb[:, s * TC + ch * CH: s * TC + (ch + 1) * CH],
                rows, scr, lgt_local, pG, pH, pE_, pF_)
            lgd = c.cV.n

            def store(sync):
                sync.wait_ge(c.cV.h, lgd)
                c.sY.dma(sync.dma_start(
                    out=lg_out.rearrange("(j r) e -> r j e", r=128),
                    in_=lgt_local[:].rearrange("p (j e) -> p j e", e=8)))
                c.sY.dma(sync.dma_start(
                    out=x2_out.rearrange("(s p) t -> p s t", p=128),
                    in_=x2_sb[:].rearrange("p (s t) -> p s t", s=NF)))
                sync.wait_ge(c.sY.h, c.sY.n)
            c.Y(store)
        else:
            barrier(c)

            def store(sync):
                c.sY.dma(sync.dma_start(
                    out=y_out.rearrange("(s p) t -> p s t", p=128),
                    in_=x2_sb[:].rearrange("p (s t) -> p s t", s=NF)))
                sync.wait_ge(c.sY.h, c.sY.n)
            c.Y(store)
    return nc


_IDS = None


def _ids():
    global _IDS
    if _IDS is None:
        _IDS = []
        for cc in range(8):
            _IDS.append(np.concatenate(
                [np.arange((cc + 8 * k) * 128, (cc + 8 * k) * 128 + 128)
                 for k in range(NT)]))
    return _IDS


def kernel(L0_attn_x, L0_attn_Wq, L0_attn_bq, L0_attn_Wk, L0_attn_bk,
           L0_attn_Wv, L0_attn_bv, L0_attn_weights, L0_ffn_expert_embeddings,
           L0_ffn_first_linear, L0_ffn_second_linear):
    xf = np.ascontiguousarray(np.asarray(L0_attn_x, dtype=np.float32)).reshape(TOK, D)
    ids = _ids()
    ident = np.eye(128, dtype=np.float32)
    strictU = np.triu(np.ones((128, 128), dtype=np.float32), k=1)
    iota8 = np.broadcast_to(np.arange(8, dtype=np.float32), (128, 8)).copy()
    eeT = np.ascontiguousarray(
        np.asarray(L0_ffn_expert_embeddings, dtype=np.float32).T)

    nc1 = build_L1()
    ins1 = [dict(x_fm=np.ascontiguousarray(xf[ids[cc]].T), eeT=eeT, ident=ident)
            for cc in range(8)]
    r1 = run_bass_kernel_spmd(nc1, ins1, core_ids=list(range(8)))
    full1 = np.empty((TOK, E), dtype=np.float32)
    for cc in range(8):
        full1[ids[cc]] = np.asarray(r1.results[cc]["lg_out"])

    W1T = np.ascontiguousarray(
        np.asarray(L0_ffn_first_linear, dtype=np.float32).transpose(0, 2, 1))
    W2T = np.ascontiguousarray(
        np.asarray(L0_ffn_second_linear, dtype=np.float32).transpose(0, 2, 1))
    gidx = []
    for cc in range(8):
        gidx.append(np.ascontiguousarray(
            ids[cc].reshape(NT, 128).T.astype(np.int32)))

    WqT = np.ascontiguousarray(np.asarray(L0_attn_Wq, dtype=np.float32).T)
    WkT = np.ascontiguousarray(np.asarray(L0_attn_Wk, dtype=np.float32).T)
    WvT = np.ascontiguousarray(np.asarray(L0_attn_Wv, dtype=np.float32).T)
    bq = np.ascontiguousarray(
        np.asarray(L0_attn_bq, dtype=np.float32).reshape(NF, 128).T)
    bk = np.ascontiguousarray(
        np.asarray(L0_attn_bk, dtype=np.float32).reshape(NF, 128).T)
    bv = np.ascontiguousarray(
        np.asarray(L0_attn_bv, dtype=np.float32).reshape(NF, 128).T)
    aw = np.asarray(L0_attn_weights, dtype=np.float32)
    whT = np.zeros((128, HEADS * HD), dtype=np.float32)
    for h in range(HEADS):
        whT[0:HD, h * HD:(h + 1) * HD] = aw[h].T
        whT[HD:128, h * HD:(h + 1) * HD] = aw[h].T
    blkT = np.zeros((128, NF * HEADS), dtype=np.float32)
    for s in range(NF):
        for r in range(128):
            h = 2 * s + (r // 64)
            blkT[r, s * 16 + h] = 1.0
    blk16 = np.zeros((HEADS, D), dtype=np.float32)
    for h in range(HEADS):
        blk16[h, h * HD:(h + 1) * HD] = 1.0

    nc2 = build_L23(True)
    ins2 = [dict(x_fm=np.ascontiguousarray(xf[ids[cc]].T), lgt_full=full1,
                 W1T=W1T, W2T=W2T, ident=ident, strictU=strictU, iota8=iota8,
                 gidx=gidx[cc], WqT=WqT, WkT=WkT, WvT=WvT, bq=bq, bk=bk, bv=bv,
                 whT=whT, blkT=blkT, blk16=blk16, eeT=eeT)
            for cc in range(8)]
    r2 = run_bass_kernel_spmd(nc2, ins2, core_ids=list(range(8)))
    full2 = np.empty((TOK, E), dtype=np.float32)
    x2 = []
    for cc in range(8):
        full2[ids[cc]] = np.asarray(r2.results[cc]["lg_out"])
        x2.append(np.ascontiguousarray(np.asarray(r2.results[cc]["x2_out"])))

    nc3 = build_L23(False)
    ins3 = [dict(x_fm=x2[cc], lgt_full=full2, W1T=W1T, W2T=W2T, ident=ident,
                 strictU=strictU, iota8=iota8, gidx=gidx[cc])
            for cc in range(8)]
    r3 = run_bass_kernel_spmd(nc3, ins3, core_ids=list(range(8)))
    y = np.empty((TOK, D), dtype=np.float32)
    for cc in range(8):
        y[ids[cc]] = np.asarray(r3.results[cc]["y_out"]).T
    return y.reshape(16, 2048, D)


# version 12 (final)
# speedup vs baseline: 1.0145x; 1.0145x over previous
import numpy as np
import concourse.bass as bass
import concourse.mybir as mybir
from concourse.bass_utils import run_bass_kernel_spmd
from contextlib import ExitStack

F32 = mybir.dt.float32
BF16 = mybir.dt.bfloat16
I32 = mybir.dt.int32
AF = mybir.ActivationFunctionType
ALU = mybir.AluOpType
AX = mybir.AxisListType
IOA = bass.IndirectOffsetOnAxis

D = 1024
NF = 8
TC = 4096
NT = 32
CH = 512
NCH = 8
E = 8
DI = 2048
NI = 16
CAP = 4096
PAD = 640
NST = PAD // 128
TOK = 32768
HEADS = 16
HD = 64
EPS = 1e-5
NEG = -1.0e30
OOB = 1.0e9


class Sem:
    def __init__(self, stack, nc, name):
        self.h = stack.enter_context(nc.semaphore(name))
        self.n = 0

    def inc(self, instr, k=1):
        instr.then_inc(self.h, k)
        self.n += k
        return self.n

    def dma(self, instr):
        return self.inc(instr, 16)


class Ctx:
    pass


_CHAIN_OPS = {
    "memset", "memzero", "tensor_copy", "tensor_tensor", "tensor_add",
    "tensor_sub", "tensor_mul", "tensor_max", "tensor_scalar",
    "tensor_single_scalar", "tensor_scalar_add", "tensor_scalar_sub",
    "tensor_scalar_mul", "tensor_scalar_max", "tensor_scalar_min",
    "tensor_relu", "tensor_reduce", "reduce_max", "reduce_sum",
    "scalar_tensor_tensor", "tensor_tensor_scan", "reciprocal", "select",
    "engine_nop", "activation", "copy", "mul", "add", "square", "sqrt",
    "sign", "iota",
}


class EP:

    def __init__(self, eng, sem):
        self._eng = eng
        self._sem = sem

    def __getattr__(self, name):
        fn = getattr(self._eng, name)
        if name not in _CHAIN_OPS:
            return fn
        sem = self._sem
        eng = self._eng

        def wrapped(*a, **k):
            if sem.n > 0:
                eng.wait_ge(sem.h, sem.n)
            ins = fn(*a, **k)
            sem.inc(ins, 1)
            return ins
        return wrapped


def mk_ctx(nc, st, block):
    c = Ctx()
    c.nc = nc
    c.T = block.tensor
    c.G = block.gpsimd
    c.Y = block.sync
    c.sY = Sem(st, nc, "sY")
    c.sA = Sem(st, nc, "sA")
    c.sG = Sem(st, nc, "sG")
    c.cP = Sem(st, nc, "cP")
    c.cV = Sem(st, nc, "cV")
    c.cA = Sem(st, nc, "cA")

    def V(f):
        block.vector(lambda eng: f(EP(eng, c.cV)))

    def A(f):
        block.scalar(lambda eng: f(EP(eng, c.cA)))
    c.V = V
    c.A = A
    return c


def barrier(c):
    snap = [(s.h, s.n) for s in (c.sY, c.sA, c.sG, c.cP, c.cV, c.cA)]

    def w(eng):
        for h, n in snap:
            if n > 0:
                eng.wait_ge(h, n)
    c.T(w)
    c.V(w)
    c.A(w)
    c.G(w)
    c.Y(w)


def emit_stats(c, xap, rows, scr, pG, pH, pre_waits=()):
    pe0 = c.cP.n
    aq0 = c.cA.n

    def act_sq(scalar):
        for sem, val in pre_waits:
            scalar.wait_ge(sem.h, val)
        for s in range(NF):
            if s >= 2:
                scalar.wait_ge(c.cP.h, pe0 + 1 + s - 1)
            scalar.activation(scr[s % 2][:], xap(s), AF.Square)
    c.A(act_sq)
    sq_ca = c.cA.n

    def pe_stats(tensor):
        for sem, val in pre_waits:
            tensor.wait_ge(sem.h, val)
        for s in range(NF):
            ins = tensor.matmul(pG[0:1, 0:CH], c.ones_col[:, 0:1], xap(s),
                                start=(s == 0), stop=(s == NF - 1))
        c.cP.inc(ins)
        for s in range(NF):
            tensor.wait_ge(c.cA.h, aq0 + s + 1)
            c.cP.inc(tensor.matmul(pH[0:1, 0:CH], c.ones_col[:, 0:1],
                                   scr[s % 2][:],
                                   start=(s == 0), stop=(s == NF - 1)))
    c.T(pe_stats)
    pe_done = c.cP.n

    def act_evac(scalar):
        scalar.wait_ge(c.cP.h, pe_done)
        scalar.copy(rows["sum"], pG[0:1, 0:CH])
        scalar.activation(rows["mean"], pG[0:1, 0:CH], AF.Copy, scale=1.0 / D)
        scalar.copy(rows["sumsq"], pH[0:1, 0:CH])
    c.A(act_evac)
    a1 = c.cA.n

    def v_var(vector):
        vector.wait_ge(c.cA.h, a1)
        vector.tensor_scalar_mul(rows["tmp"], rows["sumsq"], 1.0 / D)
        vector.tensor_tensor(rows["tmp2"], rows["mean"], rows["mean"], ALU.mult)
        vector.tensor_sub(rows["tmp"], rows["tmp"], rows["tmp2"])
    c.V(v_var)
    v1 = c.cV.n

    def act_rstd(scalar):
        scalar.wait_ge(c.cV.h, v1)
        scalar.activation(rows["tmp2"], rows["tmp"], AF.Ln, bias=c.epsb)
        scalar.activation(rows["rstd"], rows["tmp2"], AF.Exp, scale=-0.5)
    c.A(act_rstd)
    a2 = c.cA.n

    def v_mr(vector):
        vector.wait_ge(c.cA.h, a2)
        vector.scalar_tensor_tensor(rows["negmrstd"], rows["mean"], -1.0,
                                             rows["rstd"], ALU.mult, ALU.mult)
    c.V(v_mr)
    return dict(cv=c.cV.n, sq=sq_ca, evac=a1, stats_pe=pe_done)


def emit_ln_apply(c, xap, rows, xn, stats_cv, pE, pF, pe_pre=(), v_pre=()):

    def pe_b(tensor):
        tensor.wait_ge(c.cV.h, stats_cv)
        for sem, val in pe_pre:
            tensor.wait_ge(sem.h, val)
        c.cP.inc(tensor.matmul(pE[:, 0:CH], c.ones_row[0:1, :], rows["rstd"],
                               start=True, stop=True))
        c.cP.inc(tensor.matmul(pF[:, 0:CH], c.ones_row[0:1, :],
                               rows["negmrstd"], start=True, stop=True))
    c.T(pe_b)
    p1 = c.cP.n

    def v_ap(vector):
        vector.wait_ge(c.cP.h, p1)
        for sem, val in v_pre:
            vector.wait_ge(sem.h, val)
        for s in range(NF):
            vector.tensor_tensor(c.t1[:], xap(s), pE[:, 0:CH], ALU.mult)
            vector.tensor_tensor(xn[:, s * CH:(s + 1) * CH], c.t1[:],
                                 pF[:, 0:CH], ALU.add)
        vector.engine_nop()
    c.V(v_ap)
    return c.cV.n


def emit_logits(c, xap, ch, rows, stats_cv, lgt_local, pH, pI, pJ):
    v0 = c.cV.n

    def pe_t(tensor):
        tensor.wait_ge(c.cV.h, stats_cv)
        tensor.wait_ge(c.cV.h, v0)
        for s in range(NF):
            tensor.matmul(pH[0:8, 0:CH], c.eeT[:, s * 8:(s + 1) * 8],
                          xap(s), start=(s == 0), stop=False)
        c.cP.inc(tensor.matmul(pH[0:8, 0:CH], c.negs8[0:1, 0:8],
                               rows["mean"], start=False, stop=True))
        c.cP.inc(tensor.matmul(pI[0:8, 0:CH], c.ones_row[0:1, 0:8],
                               rows["rstd"], start=True, stop=True))
    c.T(pe_t)
    p1 = c.cP.n

    def v_t(vector):
        vector.wait_ge(c.cP.h, p1)
        vector.tensor_copy(c.ltmp[0:8, CH:2 * CH], pI[0:8, 0:CH])
        vector.tensor_tensor(c.ltmp[0:8, 0:CH], pH[0:8, 0:CH],
                             c.ltmp[0:8, CH:2 * CH], ALU.mult)
    c.V(v_t)
    v1 = c.cV.n

    for jt in range(4):
        tile = ch * 4 + jt

        def pe_tr(tensor, jt=jt, v1=v1):
            tensor.wait_ge(c.cV.h, v1)
            if jt > 0:
                tensor.wait_ge(c.cV.h, c.cV.n)
            c.cP.inc(tensor.transpose(
                pJ[:, 0:8], c.ltmp[0:8, jt * 128:(jt + 1) * 128],
                c.identf[0:8, 0:8]))
        c.T(pe_tr)
        p2 = c.cP.n

        def v_c(vector, tile=tile, p2=p2):
            vector.wait_ge(c.cP.h, p2)
            vector.tensor_copy(lgt_local[:, tile * 8:(tile + 1) * 8],
                               pJ[:, 0:8])
        c.V(v_c)
    return dict(cv=c.cV.n, pe=c.cP.n)


def emit_routing(c, rt, pG, t16_d):
    lgt_v = rt["lgt"][:].rearrange("p (j e) -> p j e", e=8)

    def v1(vector):
        vector.memset(rt["ones256"][:], 1.0)
        vector.reduce_max(rt["m1"][:], lgt_v, axis=AX.X)
        m1b = rt["m1"][:].unsqueeze(2).broadcast_to([128, 256, 8])
        l2v = rt["l2"][:].rearrange("p (j e) -> p j e", e=8)
        vector.tensor_tensor(l2v, lgt_v, m1b, ALU.is_equal)
        vector.tensor_scalar_mul(rt["w"][:], rt["l2"][:], NEG)
        vector.tensor_tensor(rt["msk2"][:], rt["lgt"][:], rt["w"][:], ALU.add)
        m2v = rt["msk2"][:].rearrange("p (j e) -> p j e", e=8)
        vector.reduce_max(rt["m2"][:], m2v, axis=AX.X)
        vector.tensor_sub(rt["d21"][:], rt["m2"][:], rt["m1"][:])
    c.V(v1)
    m1 = c.cV.n

    def a1(scalar):
        scalar.wait_ge(c.cV.h, m1)
        scalar.activation(rt["e21"][:], rt["d21"][:], AF.Exp)
    c.A(a1)
    a1n = c.cA.n

    def v2(vector):
        vector.wait_ge(c.cA.h, a1n)
        vector.tensor_scalar_add(rt["tmp"][:], rt["e21"][:], 1.0)
        vector.reciprocal(rt["w1"][:], rt["tmp"][:])
        vector.tensor_tensor(rt["w2"][:], rt["e21"][:], rt["w1"][:], ALU.mult)
        l2v = rt["l2"][:].rearrange("p (j e) -> p j e", e=8)
        msk2v = rt["msk2"][:].rearrange("p (j e) -> p j e", e=8)
        m2b = rt["m2"][:].unsqueeze(2).broadcast_to([128, 256, 8])
        w1b = rt["w1"][:].unsqueeze(2).broadcast_to([128, 256, 8])
        w2b = rt["w2"][:].unsqueeze(2).broadcast_to([128, 256, 8])
        wv = rt["w"][:].rearrange("p (j e) -> p j e", e=8)
        vector.tensor_tensor(msk2v, msk2v, m2b, ALU.is_equal)
        vector.tensor_tensor(wv, l2v, w1b, ALU.mult)
        vector.tensor_tensor(l2v, msk2v, w2b, ALU.mult)
        vector.tensor_add(rt["w"][:], rt["w"][:], rt["l2"][:])
        vector.tensor_single_scalar(rt["l2"][:], rt["w"][:], 0.0, ALU.is_gt)
        vector.tensor_copy(rt["msk2"][:].rearrange("p (e j) -> p j e", e=8),
                           rt["l2"][:].rearrange("p (j e) -> p j e", e=8))
        for e in range(E):
            vector.tensor_tensor_scan(rt["cs2"][:, e * 256:(e + 1) * 256],
                                      rt["ones256"][:],
                                      rt["msk2"][:, e * 256:(e + 1) * 256],
                                      0.0, ALU.mult, ALU.add)
        vector.tensor_copy(
            rt["totb"][:],
            rt["cs2"][:].rearrange("p (e j) -> p e j", j=256)[:, :, 255])
    c.V(v2)
    v2n = c.cV.n

    def pe1(tensor):
        tensor.wait_ge(c.cV.h, v2n)
        c.cP.inc(tensor.matmul(pG[:, 0:8], c.strictU[:], rt["totb"][:],
                               start=True, stop=True))
    c.T(pe1)
    p1 = c.cP.n

    def a2(scalar):
        scalar.wait_ge(c.cP.h, p1)
        scalar.copy(rt["pfx"][:], pG[:, 0:8])
    c.A(a2)
    a2n = c.cA.n

    def v3(vector):
        vector.wait_ge(c.cA.h, a2n)
        for e in range(E):
            vector.tensor_scalar_add(rt["cs2"][:, e * 256:(e + 1) * 256],
                                     rt["cs2"][:, e * 256:(e + 1) * 256],
                                     rt["pfx"][:, e:e + 1])
        vector.tensor_single_scalar(rt["t2048"][:], rt["cs2"][:], float(CAP),
                                    ALU.is_le)
        vector.tensor_tensor(rt["t2048"][:], rt["t2048"][:], rt["msk2"][:],
                             ALU.mult)
        vector.tensor_copy(rt["msk2"][:].rearrange("p (j e) -> p e j", j=256),
                           rt["t2048"][:].rearrange("p (e j) -> p e j", j=256))
        vector.tensor_tensor(rt["w"][:], rt["w"][:], rt["msk2"][:], ALU.mult)
        vector.tensor_copy(rt["t2048"][:].rearrange("p (j e) -> p e j", j=256),
                           rt["cs2"][:].rearrange("p (e j) -> p e j", j=256))
        vector.engine_nop()
    c.V(v3)
    v3n = c.cV.n

    def yw(sync):
        sync.wait_ge(c.cV.h, v3n)
        c.sY.dma(sync.dma_start(
            out=t16_d.rearrange("(p j) f -> p j f", p=128)[:, :, 0:8],
            in_=rt["w"][:].rearrange("p (j e) -> p j e", e=8)))
        c.sY.dma(sync.dma_start(
            out=t16_d.rearrange("(p j) f -> p j f", p=128)[:, :, 8:16],
            in_=rt["t2048"][:].rearrange("p (j e) -> p j e", e=8)))
    c.Y(yw)
    return c.sY.n


def emit_mine(c, mn, pG, t16_d, t16_done, sg_loads):
    def g1(gpsimd):
        gpsimd.wait_ge(c.sY.h, t16_done)
        gpsimd.wait_ge(c.sG.h, sg_loads)
        for j in range(NT):
            c.sG.dma(gpsimd.indirect_dma_start(
                mn["mine"][:, j * 16:(j + 1) * 16], None,
                t16_d[:, :], IOA(c.gidx[:, j:j + 1], 0)))
    c.G(g1)
    g1n = c.sG.n

    def v1(vector):
        vector.wait_ge(c.sG.h, g1n)
        vector.memset(mn["ones32"][:], 1.0)
        mv = mn["mine"][:].rearrange("p (j f) -> p j f", f=16)
        kmv = mn["km"][:].rearrange("p (j e) -> p j e", e=8)
        vector.tensor_single_scalar(kmv, mv[:, :, 0:8], 0.0, ALU.is_gt)
        kcv = mn["kc"][:].rearrange("p (j e) -> p j e", e=8)
        vector.tensor_copy(kcv[:, :, 0:1], kmv[:, :, 0:1])
        for e in range(1, E):
            vector.tensor_tensor(kcv[:, :, e:e + 1], kcv[:, :, e - 1:e],
                                 kmv[:, :, e:e + 1], ALU.add)
        vector.tensor_copy(mn["km2"][:].rearrange("p (e j) -> p j e", e=8), kmv)
        for e in range(E):
            vector.tensor_tensor_scan(mn["sl2"][:, e * NT:(e + 1) * NT],
                                      mn["ones32"][:],
                                      mn["km2"][:, e * NT:(e + 1) * NT],
                                      0.0, ALU.mult, ALU.add)
        vector.tensor_copy(
            mn["totb"][:],
            mn["sl2"][:].rearrange("p (e j) -> p e j", j=NT)[:, :, NT - 1])
    c.V(v1)
    v1n = c.cV.n

    def pe1(tensor):
        tensor.wait_ge(c.cV.h, v1n)
        tensor.wait_ge(c.sG.h, sg_loads)
        c.cP.inc(tensor.matmul(pG[:, 0:8], c.strictU[:], mn["totb"][:],
                               start=True, stop=True))
    c.T(pe1)
    p1 = c.cP.n

    def a1(scalar):
        scalar.wait_ge(c.cP.h, p1)
        scalar.copy(mn["pfx"][:], pG[:, 0:8])
    c.A(a1)
    a1n = c.cA.n

    def v2(vector):
        vector.wait_ge(c.cA.h, a1n)
        for e in range(E):
            vector.tensor_scalar_add(mn["sl2"][:, e * NT:(e + 1) * NT],
                                     mn["sl2"][:, e * NT:(e + 1) * NT],
                                     mn["pfx"][:, e:e + 1])
        vector.tensor_copy(mn["km2"][:].rearrange("p (j e) -> p e j", j=NT),
                           mn["sl2"][:].rearrange("p (e j) -> p e j", j=NT))
        mv = mn["mine"][:].rearrange("p (j f) -> p j f", f=16)
        kmv = mn["km"][:].rearrange("p (j e) -> p j e", e=8)
        kcv = mn["kc"][:].rearrange("p (j e) -> p j e", e=8)
        slv = mn["km2"][:].rearrange("p (j e) -> p j e", e=8)
        i8 = c.iota8[:].unsqueeze(1).broadcast_to([128, NT, 8])
        selv = mn["sel"][:].rearrange("p (j e) -> p j e", e=8)
        tv = mn["t256"][:].rearrange("p (j e) -> p j e", e=8)
        for k in (1, 2):
            vector.tensor_single_scalar(selv, kcv, float(k), ALU.is_equal)
            vector.tensor_tensor(selv, selv, kmv, ALU.mult)
            vector.tensor_tensor(tv, selv, mv[:, :, 0:8], ALU.mult)
            vector.reduce_sum(mn["w%d" % k][:], tv, axis=AX.X)
            vector.tensor_scalar_mul(mn["w%d" % k][:], mn["w%d" % k][:], 0.5)
            vector.tensor_tensor(tv, selv, i8, ALU.mult)
            vector.reduce_sum(mn["ek"][:], tv, axis=AX.X)
            vector.tensor_tensor(tv, selv, slv, ALU.mult)
            vector.reduce_sum(mn["sk"][:], tv, axis=AX.X)
            vector.reduce_sum(mn["kk"][:], selv, axis=AX.X)
            vector.tensor_scalar(mn["off"][:], mn["ek"][:], float(PAD), -1.0,
                                 ALU.mult, ALU.add)
            vector.tensor_add(mn["off"][:], mn["off"][:], mn["sk"][:])
            vector.tensor_scalar(mn["t32"][:], mn["kk"][:], -OOB, OOB,
                                 ALU.mult, ALU.add)
            vector.tensor_add(mn["off"][:], mn["off"][:], mn["t32"][:])
            vector.tensor_single_scalar(mn["t32"][:], mn["sk"][:],
                                        float(PAD), ALU.is_gt)
            vector.tensor_scalar_mul(mn["t32"][:], mn["t32"][:], OOB)
            vector.tensor_add(mn["off"][:], mn["off"][:], mn["t32"][:])
            vector.tensor_single_scalar(mn["t32"][:], mn["sk"][:],
                                        float(PAD), ALU.is_le)
            vector.tensor_tensor(mn["w%d" % k][:], mn["w%d" % k][:],
                                 mn["t32"][:], ALU.mult)
            vector.tensor_copy(mn["offi%d" % k][:], mn["off"][:])
        vector.engine_nop()
    c.V(v2)


def emit_ffn_scatter(c, f, mn, xln_d, Xp_d, ywn=None, mine_cv=None, gz_n=None):
    scats = []
    for j in range(NT):
        par = j % 2

        stg = f["Xtm"] if ywn is None else f["Ybf"]

        def yl(sync, j=j, par=par, stg=stg):
            if len(scats) >= 2:
                sync.wait_ge(c.sG.h, scats[-2])
            if ywn is not None:
                sync.wait_ge(c.sY.h, ywn[j])
                if j < 2:
                    sync.wait_ge(c.sG.h, gz_n)
            c.sY.dma(sync.dma_start(
                out=stg[par][:],
                in_=xln_d.rearrange("(p j) d -> p j d", j=NT)[:, j, :]))
        c.Y(yl)
        ld = c.sY.n

        def g(gpsimd, j=j, par=par, ld=ld, stg=stg):
            gpsimd.wait_ge(c.sY.h, ld)
            if j == 0 and mine_cv is not None:
                gpsimd.wait_ge(c.cV.h, mine_cv)
            for k in (1, 2):
                c.sG.dma(gpsimd.indirect_dma_start(
                    Xp_d[:, :], IOA(mn["offi%d" % k][:, j:j + 1], 0),
                    stg[par][:], None,
                    bounds_check=c.bc_reg, oob_is_err=False))
        c.G(g)
        scats.append(c.sG.n)


def emit_ffn_experts(c, f, W1_d, W2_d, Xp_d, Yp_d, pA, pB, pC, pD, pE_, pF_,
                     w0r):
    w1_ready = [None] * E
    w2_ready = [None] * E
    mm1_done = [None] * E
    mm2_done = [None] * E
    cast_done = {}
    xbar_done = {}
    tr_done = {}
    h_ev = []
    yv = []
    w1_ready[0], w2_ready[0] = w0r
    scr = c.scr

    for e in range(E):
        for st in range(NST):
            par = st % 2

            def yl(sync, e=e, st=st, par=par):
                key = (e, st - 2)
                if key in cast_done:
                    sync.wait_ge(c.cV.h, cast_done[key])
                elif e > 0:
                    sync.wait_ge(c.cV.h, cast_done[(e - 1, st + NST - 2)])
                c.sY.dma(sync.dma_start(
                    out=f["Xtm"][par][:],
                    in_=Xp_d[e * PAD + st * 128: e * PAD + (st + 1) * 128, :]))
            c.Y(yl)
            ld = c.sY.n

            def vcst(vector, e=e, st=st, par=par, ld=ld):
                vector.wait_ge(c.sY.h, ld)
                key = (e, st - 1) if st > 0 else (e - 1, NST - 1)
                if key in tr_done:
                    vector.wait_ge(c.cP.h, tr_done[key])
                for ff in range(NF):
                    vector.tensor_copy(
                        scr[ff // 4][:, (ff % 4) * 128:(ff % 4) * 128 + 128],
                        f["Xtm"][par][:, ff * 128:(ff + 1) * 128])
            c.V(vcst)
            cast_done[(e, st)] = c.cV.n
            cn = c.cV.n

            def txb(tensor, e=e, st=st, cn=cn):
                tensor.wait_ge(c.cV.h, cn)
                key = (e, st - 1) if st > 0 else (e - 1, NST - 1)
                if key in xbar_done:
                    tensor.wait_ge(c.cV.h, xbar_done[key])
                for ff in range(NF):
                    pb = (pE_, pF_)[ff // 4]
                    c.cP.inc(tensor.transpose(
                        pb[:, (ff % 4) * 128:(ff % 4) * 128 + 128],
                        scr[ff // 4][:, (ff % 4) * 128:(ff % 4) * 128 + 128],
                        c.identf[:, 0:128]))
            c.T(txb)
            tr_done[(e, st)] = c.cP.n
            tn = c.cP.n

            def vxb(vector, e=e, st=st, tn=tn):
                vector.wait_ge(c.cP.h, tn)
                for ff in range(NF):
                    pb = (pE_, pF_)[ff // 4]
                    vector.tensor_copy(
                        f["Xfm"][:, ff * PAD + st * 128: ff * PAD + st * 128 + 128],
                        pb[:, (ff % 4) * 128:(ff % 4) * 128 + 128])
            c.V(vxb)
            xbar_done[(e, st)] = c.cV.n

        hb = f["H"][e % 2]
        for (c0, cw, ntl) in ((0, 512, 4), (512, PAD - 512, NST - 4)):
            for i in range(NI):
                pban = (pA, pB)[i % 2]

                def t1f(tensor, e=e, c0=c0, cw=cw, ntl=ntl, i=i, pban=pban):
                    if c0 == 0 and i == 0:
                        tensor.wait_ge(c.sG.h, w1_ready[e])
                    tensor.wait_ge(c.cV.h, xbar_done[(e, (c0 // 128) + ntl - 1)])
                    if len(h_ev) >= 2:
                        tensor.wait_ge(c.cV.h, h_ev[-2])
                    for ff in range(NF):
                        ins = tensor.matmul(
                            pban[:, 0:cw],
                            f["W1T"][:, ff * DI + i * 128: ff * DI + i * 128 + 128],
                            f["Xfm"][:, ff * PAD + c0: ff * PAD + c0 + cw],
                            start=(ff == 0), stop=(ff == NF - 1))
                    c.cP.inc(ins)
                c.T(t1f)
                mmd = c.cP.n

                def vr(vector, e=e, i=i, c0=c0, cw=cw, pban=pban, mmd=mmd, hb=hb):
                    vector.wait_ge(c.cP.h, mmd)
                    if e >= 2 and i == 0 and c0 == 0:
                        vector.wait_ge(c.cP.h, mm2_done[e - 2])
                    vector.tensor_relu(
                        hb[:, i * PAD + c0: i * PAD + c0 + cw], pban[:, 0:cw])
                c.V(vr)
                h_ev.append(c.cV.n)
        mm1_done[e] = c.cP.n
        h_all = c.cV.n

        if e + 1 < E:
            def gw1(gpsimd, e=e):
                gpsimd.wait_ge(c.cP.h, mm1_done[e])
                c.sG.dma(gpsimd.dma_start(
                    out=f["W1T"][:].rearrange("p (s i) -> p s i", s=NF),
                    in_=W1_d.rearrange("e (s p) i -> e p s i", p=128)[e + 1]))
            c.G(gw1)
            w1_ready[e + 1] = c.sG.n

        for st in range(NST):
            for half in range(2):
                pban = (pC, pD)[half]

                def t2f(tensor, e=e, st=st, half=half, pban=pban, h_all=h_all,
                        hb=hb):
                    if st == 0 and half == 0:
                        tensor.wait_ge(c.sG.h, w2_ready[e])
                        tensor.wait_ge(c.cV.h, h_all)
                    if len(yv) >= 2:
                        tensor.wait_ge(c.cV.h, yv[-2][0])
                    for i in range(NI):
                        ins = tensor.matmul(
                            pban[:, 0:512],
                            hb[:, i * PAD + st * 128: i * PAD + st * 128 + 128],
                            f["W2T"][:, i * D + half * 512: i * D + half * 512 + 512],
                            start=(i == 0), stop=(i == NI - 1))
                    c.cP.inc(ins)
                c.T(t2f)
                m2d = c.cP.n

                def vy(vector, st=st, half=half, pban=pban, m2d=m2d):
                    vector.wait_ge(c.cP.h, m2d)
                    if len(yv) >= 2:
                        vector.wait_ge(c.sY.h, yv[-2][1])
                    vector.tensor_copy(
                        f["Ybf"][st % 2][:, half * 512:(half + 1) * 512],
                        pban[:, 0:512])
                c.V(vy)
                ye = c.cV.n

                def ys(sync, e=e, st=st, half=half, ye=ye):
                    if half == 1:
                        sync.wait_ge(c.cV.h, ye)
                        c.sY.dma(sync.dma_start(
                            out=Yp_d[e * PAD + st * 128: e * PAD + (st + 1) * 128, :],
                            in_=f["Ybf"][st % 2][:]))
                c.Y(ys)
                yv.append((c.cV.n, c.sY.n))
        mm2_done[e] = c.cP.n

        if e + 1 < E:
            def gw2(gpsimd, e=e):
                gpsimd.wait_ge(c.cP.h, mm2_done[e])
                c.sG.dma(gpsimd.dma_start(
                    out=f["W2T"][:].rearrange("p (s o) -> p s o", s=NI),
                    in_=W2_d.rearrange("e (s p) o -> e p s o", p=128)[e + 1]))
            c.G(gw2)
            w2_ready[e + 1] = c.sG.n


def emit_gather_back(c, x_of, mn, g, Yp_d, pC, pD):
    vws = {}
    trs = {}
    for j in range(NT):
        par = j % 2

        def gg(gpsimd, j=j, par=par):
            if j >= 2:
                gpsimd.wait_ge(c.cV.h, vws[j - 2])
            for k in (1, 2):
                c.sG.dma(gpsimd.indirect_dma_start(
                    g["gb"][par][k - 1][:], None, Yp_d[:, :],
                    IOA(mn["offi%d" % k][:, j:j + 1], 0),
                    bounds_check=c.bc_reg, oob_is_err=False))
        c.G(gg)
        gd = c.sG.n

        def vw(vector, j=j, par=par, gd=gd):
            vector.wait_ge(c.sG.h, gd)
            if j >= 2:
                vector.wait_ge(c.cP.h, trs[j - 2])
            vector.tensor_scalar_mul(g["gw32"][:], g["gb"][par][0][:],
                                     mn["w1"][:, j:j + 1])
            vector.scalar_tensor_tensor(g["gwb"][par][:], g["gb"][par][1][:],
                                        mn["w2"][:, j:j + 1], g["gw32"][:],
                                        ALU.mult, ALU.add)
        c.V(vw)
        vws[j] = c.cV.n
        vn = c.cV.n

        def tgb(tensor, j=j, par=par, vn=vn):
            tensor.wait_ge(c.cV.h, vn)
            for ff in range(NF):
                pb = (pC, pD)[ff // 4]
                c.cP.inc(tensor.transpose(
                    pb[:, (ff % 4) * 128:(ff % 4) * 128 + 128],
                    g["gwb"][par][:, ff * 128:(ff + 1) * 128],
                    c.identf[:, 0:128]))
        c.T(tgb)
        trs[j] = c.cP.n
        tn = c.cP.n

        def va(vector, j=j, par=par, tn=tn):
            vector.wait_ge(c.cP.h, tn)
            for ff in range(NF):
                pb = (pC, pD)[ff // 4]
                xs = x_of(ff)
                vector.tensor_tensor(xs[:, j * 128:(j + 1) * 128],
                                     xs[:, j * 128:(j + 1) * 128],
                                     pb[:, (ff % 4) * 128:(ff % 4) * 128 + 128],
                                     ALU.add)
        c.V(va)
    barrier(c)


def emit_attn(c, x_of, rows, scr, xn, banks, ab, w, dr):
    pA, pB, pC, pD, pE_, pF_, pG, pH = banks

    def gl(gpsimd):
        c.sG.dma(gpsimd.dma_start(out=ab["whT"][:], in_=w["whT"][:, :]))
        c.sG.dma(gpsimd.dma_start(out=ab["blkT"][:], in_=w["blkT"][:, :]))
    c.G(gl)

    def yl(sync):
        c.sY.dma(sync.dma_start(out=ab["blk16"][:], in_=w["blk16"][:, :]))
        c.sY.dma(sync.dma_start(out=ab["bias"][:, 0:NF], in_=w["bq"][:, :]))
        c.sY.dma(sync.dma_start(out=ab["bias"][:, NF:2 * NF], in_=w["bk"][:, :]))
        c.sY.dma(sync.dma_start(out=ab["bias"][:, 2 * NF:3 * NF], in_=w["bv"][:, :]))
    c.Y(yl)
    barrier(c)

    for ch in range(NCH):
        xap = (lambda s, ch=ch: x_of(s)[:, ch * CH:(ch + 1) * CH])
        sm = emit_stats(c, xap, rows, scr, pG, pH)
        lcv = emit_ln_apply(c, xap, rows, xn, sm["cv"], pE_, pF_)

        def yx(sync, ch=ch, lcv=lcv):
            sync.wait_ge(c.cV.h, lcv)
            c.sY.dma(sync.dma_start(
                out=dr["xn2"][:, ch * NF * CH:(ch + 1) * NF * CH], in_=xn[:]))
        c.Y(yx)
        barrier(c)

    for qi, wkey, zkey, bof in ((0, "WqT", "zq", 0), (1, "WkT", "zk", 1)):
        def gw(gpsimd, wkey=wkey):
            c.sG.dma(gpsimd.dma_start(
                out=ab["Wbuf"][:].rearrange("p (s o) -> p s o", s=NF),
                in_=w[wkey].rearrange("(s p) o -> p s o", p=128)))
        c.G(gw)
        barrier(c)
        for ch in range(NCH):
            def yx2(sync, ch=ch):
                c.sY.dma(sync.dma_start(
                    out=xn[:], in_=dr["xn2"][:, ch * NF * CH:(ch + 1) * NF * CH]))
            c.Y(yx2)
            xl = c.sY.n
            aev = []
            for s in range(NF):
                pban = (pA, pB)[s % 2]

                def tq(tensor, s=s, pban=pban, xl=xl):
                    tensor.wait_ge(c.sY.h, xl)
                    if len(aev) >= 2:
                        tensor.wait_ge(c.cV.h, aev[-2])
                    for ff in range(NF):
                        ins = tensor.matmul(
                            pban[:, 0:CH],
                            ab["Wbuf"][:, ff * D + s * 128: ff * D + s * 128 + 128],
                            xn[:, ff * CH:(ff + 1) * CH],
                            start=(ff == 0), stop=(ff == NF - 1))
                    c.cP.inc(ins)
                c.T(tq)
                mq = c.cP.n

                def ae(vector, s=s, bof=bof, pban=pban, mq=mq):
                    vector.wait_ge(c.cP.h, mq)
                    vector.tensor_scalar_add(
                        ab["qk"][:, s * CH:(s + 1) * CH], pban[:, 0:CH],
                        ab["bias"][:, bof * NF + s: bof * NF + s + 1])
                c.V(ae)
                aev.append(c.cV.n)
            evd = c.cV.n

            def asq(scalar, ch=ch, qi=qi, evd=evd):
                scalar.wait_ge(c.cV.h, evd)
                for s in range(NF):
                    col = qi * NF * NCH + ch * NF + s
                    ins = scalar.activation(ab["jnk"][:],
                                            ab["qk"][:, s * CH:(s + 1) * CH],
                                            AF.Square,
                                            accum_out=ab["racc"][:, col:col + 1])
                ins
            c.A(asq)

            vz = []
            for h in range(HEADS):
                def tz(tensor, h=h, evd=evd):
                    tensor.wait_ge(c.cV.h, evd)
                    if vz:
                        tensor.wait_ge(c.cV.h, vz[-1])
                    base = (h % 2) * 64
                    c.cP.inc(tensor.matmul(
                        pC[0:64, 0:CH],
                        ab["whT"][base:base + 64, h * HD:(h + 1) * HD],
                        ab["qk"][base:base + 64,
                                 (h // 2) * CH:(h // 2) * CH + CH],
                        start=True, stop=True))
                c.T(tz)
                mz = c.cP.n

                def vzf(vector, h=h, mz=mz):
                    vector.wait_ge(c.cP.h, mz)
                    base = (h % 2) * 64
                    vector.tensor_copy(
                        ab["zbuf"][base:base + 64, (h // 2) * CH:(h // 2) * CH + CH],
                        pC[0:64, 0:CH])
                c.V(vzf)
                vz.append(c.cV.n)
            zn = c.cV.n

            def yz(sync, ch=ch, zkey=zkey, zn=zn):
                sync.wait_ge(c.cV.h, zn)
                c.sY.dma(sync.dma_start(
                    out=dr[zkey][:, ch * NF * CH:(ch + 1) * NF * CH],
                    in_=ab["zbuf"][:]))
            c.Y(yz)
            barrier(c)

    def vn(vector):
        vector.reduce_sum(c.statcol[:, 0:1],
                          ab["racc"][:].rearrange("p (q x) -> p q x", q=2)[:, 0, :],
                          axis=AX.X)
        vector.reduce_sum(
            c.statcol[:, 1:2],
            ab["racc"][:].rearrange("p (q x) -> p q x", q=2)[:, 1, :], axis=AX.X)
    c.V(vn)
    v1 = c.cV.n

    def tn(tensor):
        tensor.wait_ge(c.cV.h, v1)
        c.cP.inc(tensor.matmul(pG[0:1, 0:2], c.ones_col[:, 0:1],
                               c.statcol[:, 0:2], start=True, stop=True))
    c.T(tn)
    p1 = c.cP.n

    def an(scalar):
        scalar.wait_ge(c.cP.h, p1)
        scalar.copy(ab["nrm"][0:1, 0:2], pG[0:1, 0:2])
        scalar.activation(ab["nrm"][0:1, 2:4], ab["nrm"][0:1, 0:2], AF.Ln,
                          scale=8.0)
        scalar.activation(ab["nrm"][0:1, 4:6], ab["nrm"][0:1, 2:4],
                                   AF.Exp, scale=-0.5)
    c.A(an)
    a1 = c.cA.n

    def vc(vector):
        vector.wait_ge(c.cA.h, a1)
        vector.tensor_tensor(ab["nrm"][0:1, 6:8], ab["nrm"][0:1, 4:6],
                             ab["nrm"][0:1, 4:6], ALU.mult)
        vector.tensor_tensor(ab["nrm"][0:1, 6:8], ab["nrm"][0:1, 6:8],
                             ab["nrm"][0:1, 0:2], ALU.mult)
        vector.tensor_scalar_mul(ab["nrm"][0:1, 6:8], ab["nrm"][0:1, 6:8], 8.0)
        vector.tensor_add(ab["nrm"][0:1, 2:3], ab["nrm"][0:1, 6:7],
                                   ab["nrm"][0:1, 7:8])
    c.V(vc)
    v2 = c.cV.n

    def ac(scalar):
        scalar.wait_ge(c.cV.h, v2)
        scalar.activation(ab["nrm"][0:1, 3:4], ab["nrm"][0:1, 2:3],
                                   AF.Exp, scale=-0.5)
    c.A(ac)
    a2 = c.cA.n

    def vc2(vector):
        vector.wait_ge(c.cA.h, a2)
        vector.tensor_scalar_mul(ab["nrm"][0:1, 3:4],
                                          ab["nrm"][0:1, 3:4], 1.0 / 64.0)
    c.V(vc2)
    v3 = c.cV.n

    def tb(tensor):
        tensor.wait_ge(c.cV.h, v3)
        c.cP.inc(tensor.matmul(pG[:, 16:18], c.ones_row[0:1, :],
                               ab["nrm"][0:1, 4:6], start=True, stop=True))
        c.cP.inc(tensor.matmul(pH[0:16, 0:1], c.ones_row[0:1, 0:16],
                               ab["nrm"][0:1, 3:4], start=True, stop=True))
    c.T(tb)
    p2 = c.cP.n

    def ab2(scalar):
        scalar.wait_ge(c.cP.h, p2)
        scalar.copy(ab["rcol"][:, 0:2], pG[:, 16:18])
        scalar.copy(ab["ccol"][0:16, 0:1], pH[0:16, 0:1])
    c.A(ab2)
    barrier(c)

    def gwv(gpsimd):
        c.sG.dma(gpsimd.dma_start(
            out=ab["Wbuf"][:].rearrange("p (s o) -> p s o", s=NF),
            in_=w["WvT"].rearrange("(s p) o -> p s o", p=128)))
    c.G(gwv)
    barrier(c)

    for ch in range(NCH):
        def yb(sync, ch=ch):
            c.sY.dma(sync.dma_start(
                out=xn[:], in_=dr["xn2"][:, ch * NF * CH:(ch + 1) * NF * CH]))
            c.sY.dma(sync.dma_start(
                out=ab["qk"][:], in_=dr["zq"][:, ch * NF * CH:(ch + 1) * NF * CH]))
            c.sY.dma(sync.dma_start(
                out=ab["zbuf"][:], in_=dr["zk"][:, ch * NF * CH:(ch + 1) * NF * CH]))
        c.Y(yb)
        xl = c.sY.n

        vev = []
        for s in range(NF):
            pban = (pA, pB)[s % 2]

            def tv(tensor, s=s, pban=pban, xl=xl):
                tensor.wait_ge(c.sY.h, xl)
                if len(vev) >= 2:
                    tensor.wait_ge(c.cV.h, vev[-2])
                for ff in range(NF):
                    ins = tensor.matmul(
                        pban[:, 0:CH],
                        ab["Wbuf"][:, ff * D + s * 128: ff * D + s * 128 + 128],
                        xn[:, ff * CH:(ff + 1) * CH],
                        start=(ff == 0), stop=(ff == NF - 1))
                c.cP.inc(ins)
            c.T(tv)
            mv = c.cP.n

            def av(vector, s=s, pban=pban, mv=mv):
                vector.wait_ge(c.cP.h, mv)
                vector.tensor_scalar_add(
                    ab["vbuf"][:, s * CH:(s + 1) * CH], pban[:, 0:CH],
                    ab["bias"][:, 2 * NF + s: 2 * NF + s + 1])
            c.V(av)
            vev.append(c.cV.n)
        vvd = c.cV.n

        def vu(vector, xl=xl):
            vector.wait_ge(c.sY.h, xl)
            vector.wait_ge(c.cV.h, vvd)
            for s in range(NF):
                vector.tensor_scalar_mul(c.t1[:], ab["qk"][:, s * CH:(s + 1) * CH],
                                         ab["rcol"][:, 0:1])
                vector.scalar_tensor_tensor(scr[0][:],
                                            ab["zbuf"][:, s * CH:(s + 1) * CH],
                                            ab["rcol"][:, 1:2], c.t1[:],
                                            ALU.mult, ALU.add)
                vector.tensor_copy(xn[:, s * CH:(s + 1) * CH], scr[0][:])
            vector.engine_nop()
        c.V(vu)
        vun = c.cV.n

        def aexp(scalar, vun=vun):
            scalar.wait_ge(c.cV.h, vun)
            for s in range(NF):
                ins = scalar.activation(ab["qk"][:, s * CH:(s + 1) * CH],
                                        xn[:, s * CH:(s + 1) * CH], AF.Exp)
            ins
        c.A(aexp)
        axn = c.cA.n

        def ts(tensor, axn=axn):
            tensor.wait_ge(c.cA.h, axn)
            for s in range(NF):
                ins = tensor.matmul(pC[0:16, 0:CH],
                                    ab["blkT"][:, s * 16:(s + 1) * 16],
                                    ab["qk"][:, s * CH:(s + 1) * CH],
                                    start=(s == 0), stop=(s == NF - 1))
            c.cP.inc(ins)
        c.T(ts)
        tsn = c.cP.n

        def vs(vector, tsn=tsn):
            vector.wait_ge(c.cP.h, tsn)
            vector.tensor_copy(ab["sbuf"][:], pC[0:16, 0:CH])
            vector.tensor_scalar_mul(ab["sc"][:], ab["sbuf"][:], ab["ccol"][0:16, 0:1])
            vector.reciprocal(ab["ratio"][:], ab["sc"][:])
            vector.tensor_tensor(ab["ratio"][:], ab["ratio"][:],
                                          ab["sc"][:], ALU.mult)
        c.V(vs)
        vsn = c.cV.n

        vo = []
        for s in range(NF):
            def tr(tensor, s=s, vsn=vsn):
                tensor.wait_ge(c.cV.h, vsn)
                if vo:
                    tensor.wait_ge(c.cV.h, vo[-1])
                c.cP.inc(tensor.matmul(pE_[:, 0:CH],
                                       ab["blk16"][:, s * 128:(s + 1) * 128],
                                       ab["ratio"][:], start=True, stop=True))
            c.T(tr)
            trn = c.cP.n

            def vof(vector, s=s, ch=ch, trn=trn, vvd=vvd):
                vector.wait_ge(c.cP.h, trn)
                vector.wait_ge(c.cV.h, vvd)
                vector.tensor_tensor(c.t1[:], pE_[:, 0:CH],
                                     ab["vbuf"][:, s * CH:(s + 1) * CH], ALU.mult)
                vector.tensor_tensor(
                    x_of(s)[:, ch * CH:(ch + 1) * CH],
                    x_of(s)[:, ch * CH:(ch + 1) * CH], c.t1[:], ALU.add)
            c.V(vof)
            vo.append(c.cV.n)
        barrier(c)



def emit_logits_phase(c, x_src, rows, scr, lgt_local, pG, pH, pI, pJ,
                      xch=None, x_d=None):
    hist = {}
    for ch in range(NCH):
        par = ch % 2
        if x_src is None:
            def ldx(sync, ch=ch, par=par):
                if ch >= 2:
                    w = hist[ch - 2]
                    sync.wait_ge(c.cP.h, w["lg_pe"])
                    sync.wait_ge(c.cA.h, w["sq"])
                c.sY.dma(sync.dma_start(
                    out=xch[par][:].rearrange("p (s t) -> p s t", s=NF),
                    in_=x_d.rearrange("(s p) t -> p s t",
                                      p=128)[:, :, ch * CH:(ch + 1) * CH]))
            c.Y(ldx)
            ldm = c.sY.n
            xap = (lambda s, par=par: xch[par][:, s * CH:(s + 1) * CH])
            pre = [(c.sY, ldm)]
        else:
            xap = (lambda s, ch=ch: x_src(s, ch))
            pre = []
        if ch >= 1:
            w = hist[ch - 1]
            pre = pre + [(c.cV, w["cv"]), (c.cP, w["lg_pe"]),
                         (c.cA, w["evac"]), (c.cV, w["lg_cv"])]
        sm = emit_stats(c, xap, rows, scr, pG, pH, pre_waits=pre)
        lm = emit_logits(c, xap, ch, rows, sm["cv"], lgt_local, pH, pI, pJ)
        hist[ch] = dict(cv=sm["cv"], sq=sm["sq"], evac=sm["evac"],
                        stats_pe=sm["stats_pe"], lg_pe=lm["pe"], lg_cv=lm["cv"])
    return hist


def build_L1():
    nc = bass.Bass("TRN2", target_bir_lowering=False)
    x_d = nc.dram_tensor("x_fm", [D, TC], F32, kind="ExternalInput").ap()
    eeT_d = nc.dram_tensor("eeT", [D, E], F32, kind="ExternalInput").ap()
    ident_d = nc.dram_tensor("ident", [128, 128], F32, kind="ExternalInput").ap()
    lg_out = nc.dram_tensor("lg_out", [TC, E], F32, kind="ExternalOutput").ap()

    with ExitStack() as st:
        en = st.enter_context
        block = en(nc.Block())
        c = mk_ctx(nc, st, block)
        xch = [en(nc.sbuf_tensor("xch%d" % i, [128, NF * CH], F32))
               for i in range(2)]
        eeT = en(nc.sbuf_tensor("eeT_s", [128, NF * 8], F32))
        identf = en(nc.sbuf_tensor("identf", [128, 128], F32))
        ones_col = en(nc.sbuf_tensor("ones_col", [128, 1], F32))
        ones_row = en(nc.sbuf_tensor("ones_row", [1, 128], F32))
        negs8 = en(nc.sbuf_tensor("negs8", [1, 8], F32))
        rows_t = en(nc.sbuf_tensor("rows", [1, 5 * CH], F32))
        scr = [en(nc.sbuf_tensor("scr%d" % i, [128, CH], F32)) for i in range(2)]
        rows = {n: rows_t[0:1, i * CH:(i + 1) * CH] for i, n in enumerate(
            ["sum", "sumsq", "mean", "rstd", "negmrstd"])}
        rows["tmp"] = scr[0][0:1, 0:CH]
        rows["tmp2"] = scr[1][0:1, 0:CH]
        lgt_local = en(nc.sbuf_tensor("lgt_local", [128, NT * 8], F32))
        t1 = en(nc.sbuf_tensor("t1", [128, CH], F32))
        ltmp = en(nc.sbuf_tensor("ltmp", [8, 2 * CH], F32))
        epsb = en(nc.sbuf_tensor("epsb", [1, 1], F32))
        pG = en(nc.psum_tensor("pG", [128, 512], F32))
        pH = en(nc.psum_tensor("pH", [128, 512], F32))
        pI = en(nc.psum_tensor("pI", [128, 512], F32))
        pJ = en(nc.psum_tensor("pJ", [128, 512], F32))
        c.ones_col = ones_col[:]
        c.ones_row = ones_row
        c.eeT = eeT
        c.negs8 = negs8
        c.identf = identf
        c.t1 = t1
        c.ltmp = ltmp
        c.epsb = epsb[0:1, 0:1]

        def loads(sync):
            c.sY.dma(sync.dma_start(
                out=eeT[:].rearrange("p (s e) -> p s e", s=NF),
                in_=eeT_d.rearrange("(s p) e -> p s e", p=128)))
            c.sY.dma(sync.dma_start(out=identf[:], in_=ident_d[:, :]))
        c.Y(loads)

        def vinit(vector):
            vector.memset(ones_col[:], 1.0)
            vector.memset(epsb[:], EPS)
            vector.memset(ones_row[:], 1.0)
        c.V(vinit)
        barrier(c)

        def pe_se(tensor):
            for s in range(NF):
                ins = tensor.matmul(pG[0:1, 0:8], ones_col[:, 0:1],
                                    eeT[:, s * 8:(s + 1) * 8],
                                    start=(s == 0), stop=(s == NF - 1))
            c.cP.inc(ins)
        c.T(pe_se)
        p1 = c.cP.n

        def v_negs(vector):
            vector.wait_ge(c.cP.h, p1)
            vector.tensor_scalar_mul(negs8[0:1, :], pG[0:1, 0:8], -1.0)
        c.V(v_negs)
        barrier(c)

        emit_logits_phase(c, None, rows, scr, lgt_local, pG, pH, pI, pJ,
                          xch=xch, x_d=x_d)
        lgd = c.cV.n

        def store(sync):
            sync.wait_ge(c.cV.h, lgd)
            c.sY.dma(sync.dma_start(
                out=lg_out.rearrange("(j r) e -> r j e", r=128),
                in_=lgt_local[:].rearrange("p (j e) -> p j e", e=8)))
            sync.wait_ge(c.sY.h, c.sY.n)
        c.Y(store)
    return nc


def build_L23(with_attn):
    nc = bass.Bass("TRN2", target_bir_lowering=False)
    x_d = nc.dram_tensor("x_fm", [D, TC], F32, kind="ExternalInput").ap()
    lgt_d = nc.dram_tensor("lgt_full", [TOK, E], F32, kind="ExternalInput").ap()
    W1_d = nc.dram_tensor("W1T", [E, D, DI], F32, kind="ExternalInput").ap()
    W2_d = nc.dram_tensor("W2T", [E, DI, D], F32, kind="ExternalInput").ap()
    ident_d = nc.dram_tensor("ident", [128, 128], F32, kind="ExternalInput").ap()
    triU_d = nc.dram_tensor("strictU", [128, 128], F32, kind="ExternalInput").ap()
    iota8_d = nc.dram_tensor("iota8", [128, 8], F32, kind="ExternalInput").ap()
    gidx_d = nc.dram_tensor("gidx", [128, NT], I32, kind="ExternalInput").ap()
    if with_attn:
        WqT_d = nc.dram_tensor("WqT", [D, D], F32, kind="ExternalInput").ap()
        WkT_d = nc.dram_tensor("WkT", [D, D], F32, kind="ExternalInput").ap()
        WvT_d = nc.dram_tensor("WvT", [D, D], F32, kind="ExternalInput").ap()
        bq_d = nc.dram_tensor("bq", [128, NF], F32, kind="ExternalInput").ap()
        bk_d = nc.dram_tensor("bk", [128, NF], F32, kind="ExternalInput").ap()
        bv_d = nc.dram_tensor("bv", [128, NF], F32, kind="ExternalInput").ap()
        whT_d = nc.dram_tensor("whT", [128, HEADS * HD], F32, kind="ExternalInput").ap()
        blkT_d = nc.dram_tensor("blkT", [128, NF * HEADS], F32, kind="ExternalInput").ap()
        blk16_d = nc.dram_tensor("blk16", [HEADS, D], F32, kind="ExternalInput").ap()
        eeT_d = nc.dram_tensor("eeT", [D, E], F32, kind="ExternalInput").ap()
        x2_out = nc.dram_tensor("x2_out", [D, TC], F32, kind="ExternalOutput").ap()
        lg_out = nc.dram_tensor("lg_out", [TC, E], F32, kind="ExternalOutput").ap()
    else:
        y_out = nc.dram_tensor("y_out", [D, TC], F32, kind="ExternalOutput").ap()

    xln_dr = nc.dram_tensor("xln_d", [TC, D], BF16, kind="Internal").ap()
    t16_dr = nc.dram_tensor("t16_d", [TOK, 16], F32, kind="Internal").ap()
    Xp_dr = nc.dram_tensor("Xpad_d", [E * PAD, D], BF16, kind="Internal").ap()
    Yp_dr = nc.dram_tensor("Ypad_d", [E * PAD, D], BF16, kind="Internal").ap()
    if with_attn:
        zq_dr = nc.dram_tensor("zq_d", [128, NF * TC], BF16, kind="Internal").ap()
        zk_dr = nc.dram_tensor("zk_d", [128, NF * TC], BF16, kind="Internal").ap()
        xn2_dr = nc.dram_tensor("xn2_d", [128, NF * TC], BF16, kind="Internal").ap()

    with ExitStack() as st:
        en = st.enter_context
        block = en(nc.Block())
        c = mk_ctx(nc, st, block)

        identf = en(nc.sbuf_tensor("identf", [128, 128], F32))
        strictU = en(nc.sbuf_tensor("strictU_s", [128, 128], BF16))
        iota8 = en(nc.sbuf_tensor("iota8_s", [128, 8], F32))
        gidx = en(nc.sbuf_tensor("gidx_s", [128, NT], I32))
        ones_col = en(nc.sbuf_tensor("ones_col", [128, 1], F32))
        ones_row = en(nc.sbuf_tensor("ones_row", [1, 128], F32))
        rows_t = en(nc.sbuf_tensor("rows", [1, 5 * CH], F32))
        scr = [en(nc.sbuf_tensor("scr%d" % i, [128, CH], F32)) for i in range(2)]
        rows = {n: rows_t[0:1, i * CH:(i + 1) * CH] for i, n in enumerate(
            ["sum", "sumsq", "mean", "rstd", "negmrstd"])}
        rows["tmp"] = scr[0][0:1, 0:CH]
        rows["tmp2"] = scr[1][0:1, 0:CH]
        statcol = en(nc.sbuf_tensor("statcol", [128, 2], F32))
        t1 = en(nc.sbuf_tensor("t1", [128, CH], F32))
        xn = en(nc.sbuf_tensor("xn", [128, NF * CH], BF16))
        epsb = en(nc.sbuf_tensor("epsb", [1, 1], F32))
        c.ones_col = ones_col[:]
        c.ones_row = ones_row
        c.identf = identf
        c.strictU = strictU[:]
        c.iota8 = iota8
        c.gidx = gidx
        c.statcol = statcol
        c.t1 = t1
        c.scr = scr
        c.epsb = epsb[0:1, 0:1]

        pA = en(nc.psum_tensor("pA", [128, 512], F32))
        pB = en(nc.psum_tensor("pB", [128, 512], F32))
        pC = en(nc.psum_tensor("pC", [128, 512], F32))
        pD = en(nc.psum_tensor("pD", [128, 512], F32))
        pE_ = en(nc.psum_tensor("pE", [128, 512], F32))
        pF_ = en(nc.psum_tensor("pF", [128, 512], F32))
        pG = en(nc.psum_tensor("pG", [128, 512], F32))
        pH = en(nc.psum_tensor("pH", [128, 512], F32))

        def g_loads(gpsimd):
            c.bc_reg = gpsimd.alloc_register("bc")
            gpsimd.reg_mov(c.bc_reg, E * PAD - 1)
            c.sG.dma(gpsimd.dma_start(out=gidx[:], in_=gidx_d[:, :]))
            c.sG.dma(gpsimd.dma_start(out=strictU[:], in_=triU_d[:, :]))
        c.G(g_loads)
        sg_loads = c.sG.n

        def loads(sync):
            c.sY.dma(sync.dma_start(out=identf[:], in_=ident_d[:, :]))
            c.sY.dma(sync.dma_start(out=iota8[:], in_=iota8_d[:, :]))
        c.Y(loads)

        def vinit(vector):
            vector.memset(ones_col[:], 1.0)
            vector.memset(epsb[:], EPS)
            vector.memset(ones_row[:], 1.0)
        c.V(vinit)
        barrier(c)

        mn = {}
        with ExitStack() as stm:
            enm = stm.enter_context
            mn["mine"] = enm(nc.sbuf_tensor("mine", [128, NT * 16], F32))
            mn["km"] = enm(nc.sbuf_tensor("km", [128, NT * 8], F32))
            mn["kc"] = enm(nc.sbuf_tensor("kc", [128, NT * 8], F32))
            mn["km2"] = enm(nc.sbuf_tensor("km2", [128, NT * 8], F32))
            mn["sl2"] = enm(nc.sbuf_tensor("sl2", [128, NT * 8], F32))
            mn["sel"] = enm(nc.sbuf_tensor("sel", [128, NT * 8], F32))
            mn["t256"] = enm(nc.sbuf_tensor("t256", [128, NT * 8], F32))
            mn["totb"] = enm(nc.sbuf_tensor("mtotb", [128, 8], BF16))
            mn["pfx"] = enm(nc.sbuf_tensor("mpfx", [128, 8], F32))
            mn["ones32"] = enm(nc.sbuf_tensor("ones32", [128, NT], F32))
            for nm in ["w1", "w2", "ek", "sk", "kk", "off", "t32"]:
                mn[nm] = enm(nc.sbuf_tensor("m_" + nm, [128, NT], F32))
            mn["offi1"] = enm(nc.sbuf_tensor("offi1", [128, NT], I32))
            mn["offi2"] = enm(nc.sbuf_tensor("offi2", [128, NT], I32))

            with ExitStack() as str_:
                enr = str_.enter_context
                rt = {}
                rt["lgt"] = enr(nc.sbuf_tensor("lgt", [128, 2048], F32))
                rt["l2"] = enr(nc.sbuf_tensor("rt_l2", [128, 2048], F32))
                rt["w"] = enr(nc.sbuf_tensor("rt_w", [128, 2048], F32))
                rt["msk2"] = enr(nc.sbuf_tensor("rt_msk2", [128, 2048], F32))
                rt["cs2"] = enr(nc.sbuf_tensor("rt_cs2", [128, 2048], F32))
                rt["t2048"] = enr(nc.sbuf_tensor("rt_t2048", [128, 2048], F32))
                for nm in ["m1", "m2", "d21", "e21", "w1", "w2", "tmp", "ones256"]:
                    rt[nm] = enr(nc.sbuf_tensor("rt_" + nm, [128, 256], F32))
                rt["totb"] = enr(nc.sbuf_tensor("rt_totb", [128, 8], BF16))
                rt["pfx"] = enr(nc.sbuf_tensor("rt_pfx", [128, 8], F32))

                def lgl(sync):
                    c.sY.dma(sync.dma_start(
                        out=rt["lgt"][:],
                        in_=lgt_d.rearrange("(p j) e -> p (j e)", p=128)))
                c.Y(lgl)
                barrier(c)
                t16_done = emit_routing(c, rt, pG, t16_dr)
                emit_mine(c, mn, pG, t16_dr, t16_done, sg_loads)
                mine_cv = c.cV.n
                mine_ca = c.cA.n

            with ExitStack() as stf:
                enf = stf.enter_context
                f = {}
                f["W1T"] = enf(nc.sbuf_tensor("W1T_s", [128, NF * DI], BF16))
                f["W2T"] = enf(nc.sbuf_tensor("W2T_s", [128, NI * D], BF16))
                f["Xfm"] = enf(nc.sbuf_tensor("Xfm", [128, NF * PAD], BF16))
                f["H"] = [enf(nc.sbuf_tensor("H%d" % i, [128, NI * PAD], BF16))
                          for i in range(2)]
                f["Xtm"] = [enf(nc.sbuf_tensor("fXtm%d" % i, [128, D], BF16))
                            for i in range(4)]
                f["Ybf"] = [enf(nc.sbuf_tensor("Ybf%d" % i, [128, D], BF16))
                            for i in range(2)]
                xch = [enf(nc.sbuf_tensor("xch%d" % i, [128, NF * CH], F32))
                       for i in range(2)]

                def gw0(gpsimd):
                    c.sG.dma(gpsimd.dma_start(
                        out=f["W1T"][:].rearrange("p (s i) -> p s i", s=NF),
                        in_=W1_d.rearrange("e (s p) i -> e p s i", p=128)[0]))
                    c.sG.dma(gpsimd.dma_start(
                        out=f["W2T"][:].rearrange("p (s o) -> p s o", s=NI),
                        in_=W2_d.rearrange("e (s p) o -> e p s o", p=128)[0]))
                c.G(gw0)
                w0r = (c.sG.n - 16, c.sG.n)

                def vz(vector):
                    vector.memset(f["Ybf"][0][:], 0.0)
                c.V(vz)
                vzn = c.cV.n

                def gz(gpsimd):
                    gpsimd.wait_ge(c.cV.h, vzn)
                    for b in range(E * PAD // 128):
                        c.sG.dma(gpsimd.dma_start(
                            out=Xp_dr[b * 128:(b + 1) * 128, :],
                            in_=f["Ybf"][0][:]))
                c.G(gz)
                gz_n = c.sG.n

                hist = {}
                xcop = []
                screl = []
                for ch in range(NCH):
                    par = ch % 2

                    def ldx(sync, ch=ch, par=par):
                        if ch >= 2:
                            w = hist[ch - 2]
                            sync.wait_ge(c.cP.h, w["ln_pe"])
                            sync.wait_ge(c.cA.h, w["sq"])
                            sync.wait_ge(c.cV.h, w["ln_cv"])
                        c.sY.dma(sync.dma_start(
                            out=xch[par][:].rearrange("p (s t) -> p s t", s=NF),
                            in_=x_d.rearrange("(s p) t -> p s t",
                                              p=128)[:, :, ch * CH:(ch + 1) * CH]))
                    c.Y(ldx)
                    ldm = c.sY.n
                    xap = (lambda s, par=par: xch[par][:, s * CH:(s + 1) * CH])
                    pre = [(c.sY, ldm)]
                    if ch == 0:
                        pre = pre + [(c.cA, mine_ca)]
                    if ch >= 1:
                        w = hist[ch - 1]
                        pre = pre + [(c.cV, w["cv"]), (c.cA, w["evac"]),
                                     (c.cP, w["ln_pe"])]
                    sm = emit_stats(c, xap, rows, scr, pG, pH, pre_waits=pre)
                    pe_pre = []
                    if ch >= 1:
                        pe_pre = [(c.cV, hist[ch - 1]["ln_cv"])]
                    lcv = emit_ln_apply(c, xap, rows, xch[par][:], sm["cv"],
                                        pE_, pF_, pe_pre=pe_pre,
                                        v_pre=[(c.cA, sm["sq"])])
                    for jt in range(4):
                        tile = ch * 4 + jt
                        tpar = jt

                        def txb(tensor, ch=ch, par=par, jt=jt, lcv=lcv):
                            tensor.wait_ge(c.cV.h, lcv)
                            if xcop:
                                tensor.wait_ge(c.cV.h, xcop[-1])
                            for ff in range(NF):
                                pb = (pC, pD)[ff // 4]
                                c.cP.inc(tensor.transpose(
                                    pb[:, (ff % 4) * 128:(ff % 4) * 128 + 128],
                                    xch[par][:, ff * CH + jt * 128:
                                             ff * CH + jt * 128 + 128],
                                    identf[:, 0:128]))
                        c.T(txb)
                        tn = c.cP.n

                        def vxb(vector, tile=tile, tpar=tpar, tn=tn):
                            vector.wait_ge(c.cP.h, tn)
                            if tile >= 4:
                                vector.wait_ge(c.sA.h, (tile - 4) // 4 + 1)
                            for ff in range(NF):
                                pb = (pC, pD)[ff // 4]
                                vector.tensor_copy(
                                    f["Xtm"][tpar][:, ff * 128:(ff + 1) * 128],
                                    pb[:, (ff % 4) * 128:(ff % 4) * 128 + 128])
                        c.V(vxb)
                        xcop.append(c.cV.n)
                        cn = c.cV.n

                        def g(gpsimd, tile=tile, jt=jt, tpar=tpar, cn=cn):
                            gpsimd.wait_ge(c.cV.h, cn)
                            if tile == 0:
                                gpsimd.wait_ge(c.cV.h, mine_cv)
                                gpsimd.wait_ge(c.sG.h, gz_n)
                            for k in (1, 2):
                                c.sG.dma(gpsimd.indirect_dma_start(
                                    Xp_dr[:, :],
                                    IOA(mn["offi%d" % k][:, tile:tile + 1], 0),
                                    f["Xtm"][tpar][:], None,
                                    bounds_check=c.bc_reg, oob_is_err=False))
                            if jt == 3:
                                gpsimd.wait_ge(c.sG.h, c.sG.n)
                                c.sA.inc(gpsimd.engine_nop(), 1)
                        c.G(g)
                    hist[ch] = dict(cv=sm["cv"], sq=sm["sq"], evac=sm["evac"],
                                    ln_pe=c.cP.n, ln_cv=lcv)
                barrier(c)
                emit_ffn_experts(c, f, W1_d, W2_d, Xp_dr, Yp_dr,
                                 pA, pB, pC, pD, pE_, pF_, w0r)

            x2_sb = en(nc.sbuf_tensor("x2_sb", [128, NF * TC], F32, side="right"))

            def x_of(s):
                return x2_sb[:, s * TC:(s + 1) * TC]

            def xload(sync):
                c.sY.dma(sync.dma_start(
                    out=x2_sb[:].rearrange("p (s t) -> p s t", s=NF),
                    in_=x_d.rearrange("(s p) t -> p s t", p=128)))
            c.Y(xload)

            with ExitStack() as stg:
                eng = stg.enter_context
                g = {}
                g["gb"] = [[eng(nc.sbuf_tensor("gb%d_%d" % (pp, i), [128, D], BF16))
                            for i in range(2)] for pp in range(2)]
                g["gw32"] = eng(nc.sbuf_tensor("gw32", [128, D], F32))
                g["gwb"] = [eng(nc.sbuf_tensor("gwb%d" % pp, [128, D], F32))
                            for pp in range(2)]
                barrier(c)
                emit_gather_back(c, x_of, mn, g, Yp_dr, pC, pD)

        if with_attn:
            with ExitStack() as sta:
                ena = sta.enter_context
                ab = {}
                ab["Wbuf"] = ena(nc.sbuf_tensor("aWbuf", [128, NF * D], BF16))
                ab["qk"] = ena(nc.sbuf_tensor("aqk", [128, NF * CH], BF16))
                ab["zbuf"] = ena(nc.sbuf_tensor("azbuf", [128, NF * CH], BF16))
                ab["vbuf"] = ena(nc.sbuf_tensor("avbuf", [128, NF * CH], BF16))
                ab["whT"] = ena(nc.sbuf_tensor("awhT", [128, HEADS * HD], BF16))
                ab["bias"] = ena(nc.sbuf_tensor("abias", [128, 3 * NF], F32))
                ab["racc"] = ena(nc.sbuf_tensor("aracc", [128, 2 * NF * NCH], F32))
                ab["jnk"] = ena(nc.sbuf_tensor("ajnk", [128, CH], BF16))
                ab["nrm"] = ena(nc.sbuf_tensor("anrm", [1, 8], F32))
                ab["rcol"] = ena(nc.sbuf_tensor("arcol", [128, 2], F32))
                ab["blkT"] = ena(nc.sbuf_tensor("ablkT", [128, NF * HEADS], BF16))
                ab["blk16"] = ena(nc.sbuf_tensor("ablk16", [16, D], F32))
                ab["sbuf"] = ena(nc.sbuf_tensor("asbuf", [16, CH], F32))
                ab["sc"] = ena(nc.sbuf_tensor("asc", [16, CH], F32))
                ab["ratio"] = ena(nc.sbuf_tensor("aratio", [16, CH], F32))
                ab["ccol"] = ena(nc.sbuf_tensor("accol", [16, 1], F32))
                w = dict(WqT=WqT_d, WkT=WkT_d, WvT=WvT_d, bq=bq_d, bk=bk_d,
                         bv=bv_d, whT=whT_d, blkT=blkT_d, blk16=blk16_d)
                dr = dict(zq=zq_dr, zk=zk_dr, xn2=xn2_dr)
                emit_attn(c, x_of, rows, scr, xn,
                          (pA, pB, pC, pD, pE_, pF_, pG, pH), ab, w, dr)

            eeT = en(nc.sbuf_tensor("eeT_s", [128, NF * 8], F32))
            negs8 = en(nc.sbuf_tensor("negs8", [1, 8], F32))
            lgt_local = en(nc.sbuf_tensor("lgt_local", [128, NT * 8], F32))
            ltmp = en(nc.sbuf_tensor("ltmp", [8, 2 * CH], F32))
            c.eeT = eeT
            c.negs8 = negs8
            c.ltmp = ltmp

            def eel(sync):
                c.sY.dma(sync.dma_start(
                    out=eeT[:].rearrange("p (s e) -> p s e", s=NF),
                    in_=eeT_d.rearrange("(s p) e -> p s e", p=128)))
            c.Y(eel)
            barrier(c)

            def pe_se(tensor):
                for s in range(NF):
                    ins = tensor.matmul(pG[0:1, 0:8], ones_col[:, 0:1],
                                        eeT[:, s * 8:(s + 1) * 8],
                                        start=(s == 0), stop=(s == NF - 1))
                c.cP.inc(ins)
            c.T(pe_se)
            p1 = c.cP.n

            def v_negs(vector):
                vector.wait_ge(c.cP.h, p1)
                vector.tensor_scalar_mul(negs8[0:1, :], pG[0:1, 0:8], -1.0)
            c.V(v_negs)
            barrier(c)

            emit_logits_phase(
                c, lambda s, ch: x2_sb[:, s * TC + ch * CH: s * TC + (ch + 1) * CH],
                rows, scr, lgt_local, pG, pH, pE_, pF_)
            lgd = c.cV.n

            def store(sync):
                sync.wait_ge(c.cV.h, lgd)
                c.sY.dma(sync.dma_start(
                    out=lg_out.rearrange("(j r) e -> r j e", r=128),
                    in_=lgt_local[:].rearrange("p (j e) -> p j e", e=8)))
                c.sY.dma(sync.dma_start(
                    out=x2_out.rearrange("(s p) t -> p s t", p=128),
                    in_=x2_sb[:].rearrange("p (s t) -> p s t", s=NF)))
                sync.wait_ge(c.sY.h, c.sY.n)
            c.Y(store)
        else:
            barrier(c)

            def store(sync):
                c.sY.dma(sync.dma_start(
                    out=y_out.rearrange("(s p) t -> p s t", p=128),
                    in_=x2_sb[:].rearrange("p (s t) -> p s t", s=NF)))
                sync.wait_ge(c.sY.h, c.sY.n)
            c.Y(store)
    return nc


_IDS = None


def _ids():
    global _IDS
    if _IDS is None:
        _IDS = []
        for cc in range(8):
            _IDS.append(np.concatenate(
                [np.arange((cc + 8 * k) * 128, (cc + 8 * k) * 128 + 128)
                 for k in range(NT)]))
    return _IDS


def kernel(L0_attn_x, L0_attn_Wq, L0_attn_bq, L0_attn_Wk, L0_attn_bk,
           L0_attn_Wv, L0_attn_bv, L0_attn_weights, L0_ffn_expert_embeddings,
           L0_ffn_first_linear, L0_ffn_second_linear):
    xf = np.ascontiguousarray(np.asarray(L0_attn_x, dtype=np.float32)).reshape(TOK, D)
    ids = _ids()
    ident = np.eye(128, dtype=np.float32)
    strictU = np.triu(np.ones((128, 128), dtype=np.float32), k=1)
    iota8 = np.broadcast_to(np.arange(8, dtype=np.float32), (128, 8)).copy()
    eeT = np.ascontiguousarray(
        np.asarray(L0_ffn_expert_embeddings, dtype=np.float32).T)

    nc1 = build_L1()
    ins1 = [dict(x_fm=np.ascontiguousarray(xf[ids[cc]].T), eeT=eeT, ident=ident)
            for cc in range(8)]
    r1 = run_bass_kernel_spmd(nc1, ins1, core_ids=list(range(8)))
    full1 = np.empty((TOK, E), dtype=np.float32)
    for cc in range(8):
        full1[ids[cc]] = np.asarray(r1.results[cc]["lg_out"])

    W1T = np.ascontiguousarray(
        np.asarray(L0_ffn_first_linear, dtype=np.float32).transpose(0, 2, 1))
    W2T = np.ascontiguousarray(
        np.asarray(L0_ffn_second_linear, dtype=np.float32).transpose(0, 2, 1))
    gidx = []
    for cc in range(8):
        gidx.append(np.ascontiguousarray(
            ids[cc].reshape(NT, 128).T.astype(np.int32)))

    WqT = np.ascontiguousarray(np.asarray(L0_attn_Wq, dtype=np.float32).T)
    WkT = np.ascontiguousarray(np.asarray(L0_attn_Wk, dtype=np.float32).T)
    WvT = np.ascontiguousarray(np.asarray(L0_attn_Wv, dtype=np.float32).T)
    bq = np.ascontiguousarray(
        np.asarray(L0_attn_bq, dtype=np.float32).reshape(NF, 128).T)
    bk = np.ascontiguousarray(
        np.asarray(L0_attn_bk, dtype=np.float32).reshape(NF, 128).T)
    bv = np.ascontiguousarray(
        np.asarray(L0_attn_bv, dtype=np.float32).reshape(NF, 128).T)
    aw = np.asarray(L0_attn_weights, dtype=np.float32)
    whT = np.zeros((128, HEADS * HD), dtype=np.float32)
    for h in range(HEADS):
        whT[0:HD, h * HD:(h + 1) * HD] = aw[h].T
        whT[HD:128, h * HD:(h + 1) * HD] = aw[h].T
    blkT = np.zeros((128, NF * HEADS), dtype=np.float32)
    for s in range(NF):
        for r in range(128):
            h = 2 * s + (r // 64)
            blkT[r, s * 16 + h] = 1.0
    blk16 = np.zeros((HEADS, D), dtype=np.float32)
    for h in range(HEADS):
        blk16[h, h * HD:(h + 1) * HD] = 1.0

    nc2 = build_L23(True)
    ins2 = [dict(x_fm=np.ascontiguousarray(xf[ids[cc]].T), lgt_full=full1,
                 W1T=W1T, W2T=W2T, ident=ident, strictU=strictU, iota8=iota8,
                 gidx=gidx[cc], WqT=WqT, WkT=WkT, WvT=WvT, bq=bq, bk=bk, bv=bv,
                 whT=whT, blkT=blkT, blk16=blk16, eeT=eeT)
            for cc in range(8)]
    r2 = run_bass_kernel_spmd(nc2, ins2, core_ids=list(range(8)))
    full2 = np.empty((TOK, E), dtype=np.float32)
    x2 = []
    for cc in range(8):
        full2[ids[cc]] = np.asarray(r2.results[cc]["lg_out"])
        x2.append(np.ascontiguousarray(np.asarray(r2.results[cc]["x2_out"])))

    nc3 = build_L23(False)
    ins3 = [dict(x_fm=x2[cc], lgt_full=full2, W1T=W1T, W2T=W2T, ident=ident,
                 strictU=strictU, iota8=iota8, gidx=gidx[cc])
            for cc in range(8)]
    r3 = run_bass_kernel_spmd(nc3, ins3, core_ids=list(range(8)))
    y = np.empty((TOK, D), dtype=np.float32)
    for cc in range(8):
        y[ids[cc]] = np.asarray(r3.results[cc]["y_out"]).T
    return y.reshape(16, 2048, D)
